# version 03
# baseline (speedup 1.0000x reference)
import numpy as np
import ml_dtypes
from contextlib import ExitStack

import concourse.bass as bass
import concourse.mybir as mybir
from concourse.bass_utils import run_bass_kernel_spmd

F32 = mybir.dt.float32
BF16 = mybir.dt.bfloat16
AF = mybir.ActivationFunctionType
ALU = mybir.AluOpType
AX = mybir.AxisListType

NSLOT = 8
NE = 8
DEBUG_TAPS = False
DEBUG_RES = {}
EPS = 1e-5
RSQ = float(1.0 / np.sqrt(128.0))
NEG = -1e9


class Sem:
    def __init__(self, h):
        self.h = h
        self.n = 0

    def inc(self, instr, k=1):
        instr.then_inc(self.h, k)
        self.n += k
        return self.n


class Rot:
    def __init__(self, n):
        self.n = n
        self.last = [[] for _ in range(n)]
        self.cnt = 0

    def next(self):
        i = self.cnt % self.n
        self.cnt += 1
        return i, list(self.last[i])

    def done(self, i, sem):
        self.last[i] = [(sem, sem.n)]


class Bld:
    pass


def f32v(t, c0, n):
    return t.bitcast(F32)[:, c0:c0 + n]


def _waits(eng, ws):
    for sem, val in ws:
        if val:
            eng.wait_ge(sem.h, val)


def build_program():
    nc = bass.Bass("TRN2", target_bir_lowering=False)
    B = Bld()
    B.nc = nc

    di = lambda name, shape, dt: nc.dram_tensor(name, shape, dt, kind="ExternalInput")
    B.xfull = di("xfull", [16, 128, 2048], F32)
    B.identb_d = di("identb_d", [128, 128], BF16)
    B.identf_d = di("identf_d", [128, 128], F32)
    B.masks_d = di("masks_d", [128, 2048], BF16)
    B.ropec_d = di("ropec_d", [128, 1024], F32)
    B.ropes_d = di("ropes_d", [128, 1024], F32)
    B.n1w_d = di("n1w_d", [128, 16], F32)
    B.n2w_d = di("n2w_d", [128, 16], F32)
    B.gatew_d = di("gatew_d", [128, 128], F32)
    B.wqkv_d = di("wqkv_d", [16, 3, 128, 2048], F32)
    B.wv4_d = di("wv4_d", [4, 128, 8192], F32)
    B.wout_d = di("wout_d", [4, 128, 8192], F32)
    B.wq_d = di("wq_d", [4, 128, 8192], F32)
    B.wk_d = di("wk_d", [1, 128, 8192], F32)
    B.wv_d = di("wv_d", [1, 128, 8192], F32)
    B.wo_d = di("wo_d", [4, 128, 8192], F32)
    B.w1_d = di("w1_d", [8, 32, 128, 2048], F32)
    B.w3_d = di("w3_d", [8, 32, 128, 2048], F32)
    B.w2_d = di("w2_d", [8, 4, 4, 128, 4096], F32)
    B.yout = nc.dram_tensor("yout", [8, 128, 2048], F32, kind="ExternalOutput")
    B.sdram = nc.dram_tensor("sdram", [8, 128, 2048], BF16, kind="Internal")
    if DEBUG_TAPS:
        B.s_dbg = nc.dram_tensor("s_dbg", [8, 128, 2048], BF16, kind="ExternalOutput")
        B.h_dbg = nc.dram_tensor("h_dbg", [8, 128, 2048], F32, kind="ExternalOutput")
        B.qkv_dbg = nc.dram_tensor("qkv_dbg", [128, 32768], BF16, kind="ExternalOutput")
        B.ctx_dbg = nc.dram_tensor("ctx_dbg", [128, 16384], BF16, kind="ExternalOutput")
        B.f_dbg = nc.dram_tensor("f_dbg", [8, 128, 2048], F32, kind="ExternalOutput")

    ctx = ExitStack()
    sb = lambda name, shape, dt: ctx.enter_context(nc.sbuf_tensor(name, shape, dt))
    ps = lambda name, shape, dt: ctx.enter_context(nc.psum_tensor(name, shape, dt))

    B.AR1 = sb("AR1", [128, 32768], BF16)
    B.AR2 = sb("AR2", [128, 32768], BF16)
    B.AR3 = sb("AR3", [128, 16384], BF16)
    B.AR5 = sb("AR5", [128, 8192], BF16)

    B.identb = sb("identb", [128, 128], BF16)
    B.identf = sb("identf", [128, 128], F32)
    B.n1w = sb("n1w", [128, 16], F32)
    B.n2w = sb("n2w", [128, 16], F32)
    B.gatew = sb("gatew", [128, 128], F32)
    B.rstd = sb("rstd", [128, 16], F32)
    B.bln = sb("bln", [128, 16], F32)
    B.rstd1 = sb("rstd1", [128, 16], F32)
    B.rstd1q = sb("rstd1q", [128, 16], F32)
    B.rstd2 = sb("rstd2", [128, 8], F32)
    B.stat = sb("stat", [128, 8], F32)
    B.negm = sb("negm", [128, 2], F32)
    B.rsum = sb("rsum", [128, 2], F32)
    B.rrec = sb("rrec", [128, 2], F32)
    B.logits = sb("logits", [128, 64], F32)
    B.wte = sb("wte", [128, 64], F32)
    B.v1 = sb("v1", [128, 1], F32)
    B.v2 = sb("v2", [128, 1], F32)
    B.p1c = sb("p1c", [128, 1], F32)
    B.p2c = sb("p2c", [128, 1], F32)
    B.m1 = sb("m1", [128, 8], F32)
    B.m2 = sb("m2", [128, 8], F32)
    B.lgx = sb("lgx", [128, 8], F32)
    B.xnbf_c0 = [0, 2048]
    B.qsc = sb("qsc", [128, 512], F32)
    B.rsc = sb("rsc", [128, 512], F32)
    B.qbf = sb("qbf", [128, 512], BF16)
    B.utm_c0 = [0, 2048]
    B.xnsc = sb("xnsc", [128, 512], F32)
    B.sstg = [sb("sstg0", [128, 512], BF16), sb("sstg1", [128, 512], BF16)]
    B.sld = [sb("sld0", [128, 2048], BF16), sb("sld1", [128, 2048], BF16)]
    B.gfm = [sb("gfm0", [128, 128], F32), sb("gfm1", [128, 128], F32)]

    B.PSB = ps("PSB", [128, 2048], F32)
    B.PSC = ps("PSC", [128, 1024], F32)
    B.PST = [ps("PST0", [128, 1024], BF16), ps("PST1", [128, 1024], BF16)]

    B.spe = Sem(ctx.enter_context(nc.semaphore("spe")))
    B.sdv = Sem(ctx.enter_context(nc.semaphore("sdv")))
    B.sac = Sem(ctx.enter_context(nc.semaphore("sac")))
    B.qs = Sem(ctx.enter_context(nc.semaphore("qs")))
    B.qg = Sem(ctx.enter_context(nc.semaphore("qg")))

    B.trr = Rot(2)
    B.prr = Rot(2)
    B.attn = {"exp_done": 0, "ctx_evac": [], "pn_free": 0,
              "norm_done_0": 0, "norm_done_1": 0}

    B.block = ctx.enter_context(nc.Block())

    _consts(B)
    _p0(B)
    _p5(B)
    _fence(B)
    _p1(B)
    _fence(B)
    _p2(B)
    _fence(B)
    _p3(B)
    _fence(B)
    _p4(B)

    ctx.close()
    return nc


def _fence(B):
    pe_n, dv_n, ac_n = B.spe.n, B.sdv.n, B.sac.n

    @B.block.tensor
    def _(t):
        t.wait_ge(B.sdv.h, dv_n)
        t.wait_ge(B.sac.h, ac_n)

    @B.block.vector
    def _(v):
        v.wait_ge(B.spe.h, pe_n)
        v.wait_ge(B.sac.h, ac_n)

    @B.block.scalar
    def _(s):
        s.wait_ge(B.spe.h, pe_n)
        s.wait_ge(B.sdv.h, dv_n)


def _consts(B):
    qs = B.qs

    @B.block.sync
    def _(sy):
        qs.inc(sy.dma_start(out=B.identb[:, :], in_=B.identb_d[:, :]), 16)
        qs.inc(sy.dma_start(out=B.identf[:, :], in_=B.identf_d[:, :]), 16)
        qs.inc(sy.dma_start(out=B.n1w[:, :], in_=B.n1w_d[:, :]), 16)
        qs.inc(sy.dma_start(out=B.n2w[:, :], in_=B.n2w_d[:, :]), 16)
        qs.inc(sy.dma_start(out=B.gatew[:, :], in_=B.gatew_d[:, :]), 16)
    B.m_consts = qs.n


def _p0(B):
    qs, sdv, sac, spe = B.qs, B.sdv, B.sac, B.spe
    xt = [f32v(B.AR3, 0, 2048), f32v(B.AR3, 2048, 2048)]
    xm = f32v(B.AR5, 0, 2048)
    sq = f32v(B.AR5, 2048, 2048)

    actdone = [0] * 16
    xnbf_free = [0] * 16

    for p in range(16):
        buf = p % 2

        @B.block.sync
        def _(sy, p=p, buf=buf):
            if p == 0:
                sy.wait_ge(qs.h, B.m_consts)
            if p >= 2:
                sy.wait_ge(sdv.h, actdone[p - 2])
            qs.inc(sy.dma_start(out=xt[buf], in_=B.xfull[p, :, :]), 16)
        m_load = qs.n

        @B.block.vector
        def _(v, p=p, buf=buf, m_load=m_load):
            v.wait_ge(qs.h, m_load)
            sdv.inc(v.reduce_sum(B.stat[:, 0:1], xt[buf], axis=AX.X))
            v.wait_ge(sdv.h, sdv.n)
            sdv.inc(v.tensor_scalar_mul(B.stat[:, 1:2], B.stat[:, 0:1], -1.0 / 2048.0))
        m_negm = sdv.n

        @B.block.vector
        def _(v, buf=buf, m_negm=m_negm):
            v.wait_ge(sdv.h, m_negm)
            sdv.inc(v.tensor_scalar(xm, xt[buf], B.stat[:, 1:2], None, ALU.add))
            v.wait_ge(sdv.h, sdv.n)
            sdv.inc(v.scalar_tensor_tensor(sq, xm, 1.0, xm, ALU.mult, ALU.mult,
                                           accum_out=B.stat[:, 2:3]))
        m_ss = sdv.n

        @B.block.vector
        def _(v, m_ss=m_ss):
            v.wait_ge(sdv.h, m_ss)
            sdv.inc(v.tensor_scalar(B.stat[:, 3:4], B.stat[:, 2:3], 1.0 / 2048.0, EPS,
                                    ALU.mult, ALU.add))
        m_vps = sdv.n

        @B.block.scalar
        def _(s, m_vps=m_vps):
            s.wait_ge(sdv.h, m_vps)
            sac.inc(s.activation(B.stat[:, 3:4], B.stat[:, 3:4], AF.Sqrt))
        m_sd = sac.n

        @B.block.vector
        def _(v, p=p, m_sd=m_sd):
            v.wait_ge(sac.h, m_sd)
            sdv.inc(v.reciprocal(B.rstd[:, p:p + 1], B.stat[:, 3:4]))
            v.wait_ge(sdv.h, sdv.n)
            sdv.inc(v.tensor_mul(B.bln[:, p:p + 1], B.stat[:, 1:2], B.rstd[:, p:p + 1]))
            sdv.inc(v.tensor_mul(B.stat[:, 4:5], B.rstd[:, p:p + 1], B.rstd[:, p:p + 1]))
            v.wait_ge(sdv.h, sdv.n)
            sdv.inc(v.tensor_scalar_mul(B.stat[:, 5:6], B.stat[:, 2:3], 1.0 / 2048.0))
            v.wait_ge(sdv.h, sdv.n)
            sdv.inc(v.tensor_mul(B.stat[:, 6:7], B.stat[:, 5:6], B.stat[:, 4:5]))
        m_vr = sdv.n

        @B.block.vector
        def _(v, m_vr=m_vr):
            v.wait_ge(sdv.h, m_vr)
            sdv.inc(v.tensor_scalar(B.stat[:, 7:8], B.stat[:, 6:7], EPS, None, ALU.add))
        m_vr2 = sdv.n

        @B.block.scalar
        def _(s, m_vr2=m_vr2):
            s.wait_ge(sdv.h, m_vr2)
            sac.inc(s.activation(B.stat[:, 7:8], B.stat[:, 7:8], AF.Sqrt))
        m_sd1 = sac.n

        @B.block.vector
        def _(v, p=p, m_sd1=m_sd1):
            v.wait_ge(sac.h, m_sd1)
            sdv.inc(v.reciprocal(B.rstd1[:, p:p + 1], B.stat[:, 7:8]))
            v.wait_ge(sdv.h, sdv.n)
            sdv.inc(v.tensor_scalar_mul(B.rstd1q[:, p:p + 1], B.rstd1[:, p:p + 1], RSQ))
        m_r1 = sdv.n

        @B.block.vector
        def _(v, p=p, buf=buf, m_r1=m_r1):
            v.wait_ge(sdv.h, m_r1)
            if p >= 2:
                v.wait_ge(spe.h, xnbf_free[p - 2])
            c0 = B.xnbf_c0[buf]
            sdv.inc(v.tensor_scalar(B.AR2[:, c0:c0 + 2048], xt[buf],
                                    B.rstd[:, p:p + 1], B.bln[:, p:p + 1],
                                    ALU.mult, ALU.add))
        actdone[p] = sdv.n
        m_xn = sdv.n

        for g in range(4):
            bk, war = B.trr.next()

            @B.block.tensor
            def _(t, buf=buf, g=g, bk=bk, war=war, m_xn=m_xn):
                if g == 0:
                    t.wait_ge(sdv.h, m_xn)
                _waits(t, war)
                c0 = B.xnbf_c0[buf]
                for j in range(4):
                    fc = g * 4 + j
                    mm = t.transpose(B.PST[bk][:, j * 128:(j + 1) * 128],
                                     B.AR2[:, c0 + fc * 128:c0 + (fc + 1) * 128],
                                     B.identb[:, :])
                spe.inc(mm)
            m_tr = spe.n

            @B.block.vector
            def _(v, p=p, g=g, bk=bk, m_tr=m_tr):
                v.wait_ge(spe.h, m_tr)
                dst = bass.AP(B.AR1, (g * 4) * 2048 + p * 128,
                              [[32768, 128], [2048, 4], [1, 128]])
                sdv.inc(v.tensor_copy(dst, B.PST[bk][:, 0:512]))
            B.trr.done(bk, sdv)
        xnbf_free[p] = spe.n

    B.m_p0 = {"pe": spe.n, "dv": sdv.n, "ac": sac.n}


def _attn_slots(B, qh_ap, k_ap, v_ap, ctx_dst, first_wait):
    spe, sdv, sac = B.spe, B.sdv, B.sac
    st = B.attn

    for i in range(NSLOT):
        ext = 256 * (i + 1)
        par = i % 2
        ed = st["exp_done"]

        @B.block.tensor
        def _(t, i=i, ext=ext, ed=ed, fw=(first_wait if i == 0 else [])):
            if ed:
                t.wait_ge(sac.h, ed)
            _waits(t, fw)
            lhsT = qh_ap(i)
            off = 0
            while off < ext:
                n = min(512, ext - off)
                t.matmul(B.PSB[:, off:off + n], lhsT, k_ap(off, n),
                         start=True, stop=False, skip_group_check=True)
                off += n
            mm = t.matmul(B.PSB[:, ext - 256:ext], B.identb[:, :],
                          B.AR5[:, 6144 + i * 256:6144 + (i + 1) * 256],
                          start=False, stop=True, skip_group_check=True)
            spe.inc(mm)
        m_sc = spe.n

        @B.block.vector
        def _(v, ext=ext, par=par, m_sc=m_sc, nd=st["norm_done_%d" % par]):
            v.wait_ge(spe.h, m_sc)
            if nd:
                v.wait_ge(sac.h, nd)
            sdv.inc(v.reduce_max(B.negm[:, par:par + 1], B.PSB[:, 0:ext],
                                 axis=AX.X, negate=True))
        m_max = sdv.n

        @B.block.scalar
        def _(s, ext=ext, par=par, m_max=m_max):
            s.wait_ge(sdv.h, m_max)
            sac.inc(s.activation(B.AR5[:, 0:ext], B.PSB[:, 0:ext], AF.Exp,
                                 bias=B.negm[:, par:par + 1],
                                 accum_out=B.rsum[:, par:par + 1]))
        st["exp_done"] = sac.n
        m_exp = sac.n

        @B.block.vector
        def _(v, par=par, m_exp=m_exp):
            v.wait_ge(sac.h, m_exp)
            sdv.inc(v.reciprocal(B.rrec[:, par:par + 1], B.rsum[:, par:par + 1]))
        m_rec = sdv.n

        @B.block.scalar
        def _(s, ext=ext, par=par, m_rec=m_rec, pnf=st["pn_free"]):
            s.wait_ge(sdv.h, m_rec)
            if pnf:
                s.wait_ge(B.spe.h, pnf)
            sac.inc(s.activation(B.AR5[:, 2048:2048 + ext], B.AR5[:, 0:ext],
                                 AF.Copy, scale=B.rrec[:, par:par + 1]))
        st["norm_done_%d" % par] = sac.n
        m_norm = sac.n

        ntile = 2 * (i + 1)
        ngrp = (ntile + 3) // 4
        m_pv = 0
        for g in range(ngrp):
            j0, j1 = g * 4, min(g * 4 + 4, ntile)
            bk, war = B.trr.next()

            @B.block.tensor
            def _(t, g=g, j0=j0, j1=j1, bk=bk, war=war, m_norm=m_norm):
                if g == 0:
                    t.wait_ge(sac.h, m_norm)
                _waits(t, war)
                for j in range(j0, j1):
                    mm = t.transpose(B.PST[bk][:, (j - j0) * 128:(j - j0 + 1) * 128],
                                     B.AR5[:, 2048 + j * 128:2048 + (j + 1) * 128],
                                     B.identb[:, :])
                spe.inc(mm)
            m_tr = spe.n

            @B.block.vector
            def _(v, j0=j0, j1=j1, bk=bk, m_tr=m_tr):
                v.wait_ge(spe.h, m_tr)
                sdv.inc(v.tensor_copy(B.AR5[:, 4096 + j0 * 128:4096 + j1 * 128],
                                      B.PST[bk][:, 0:(j1 - j0) * 128]))
            B.trr.done(bk, sdv)
            m_ev = sdv.n

            @B.block.tensor
            def _(t, g=g, j0=j0, j1=j1, ntile=ntile, m_ev=m_ev,
                  cw=(list(st["ctx_evac"]) if g == 0 else [])):
                t.wait_ge(sdv.h, m_ev)
                _waits(t, cw)
                for j in range(j0, j1):
                    mm = t.matmul(B.PSC[:, 0:128], v_ap(j),
                                  B.AR5[:, 4096 + j * 128:4096 + (j + 1) * 128],
                                  start=(j == 0), stop=(j == ntile - 1),
                                  skip_group_check=True)
                spe.inc(mm)
            m_pv = spe.n
        st["pn_free"] = m_pv

        @B.block.vector
        def _(v, i=i, m_pv=m_pv):
            v.wait_ge(spe.h, m_pv)
            tns, col = ctx_dst(i)
            sdv.inc(v.tensor_copy(tns[:, col:col + 128], B.PSC[:, 0:128]))
        st["ctx_evac"] = [(sdv, sdv.n)]


def _p5(B):
    qs, qg, spe, sdv, sac = B.qs, B.qg, B.spe, B.sdv, B.sac
    xn = B.AR1
    C_WQ = [0, 2048]
    C_WK = [4096, 6144]
    C_WV4 = 8192
    C_KH = 16384
    C_V4 = 18432
    C_QH = 26624

    @B.block.sync
    def _(sy):
        sy.wait_ge(sdv.h, B.m_p0["dv"])
        qs.inc(sy.dma_start(out=B.AR5[:, 6144:8192], in_=B.masks_d[:, :]), 16)
    m_masks = qs.n

    prod_mm_done = [0] * 16
    prod_ev_done = [0] * 16
    grp_v4_mm = [0] * 4
    B._s_stored = [0, 0]

    def punit(B, mm_fn, evac_dst, evac_scale, first_ws):
        half, war = B.prr.next()
        c0 = half * 512

        @B.block.tensor
        def _(t, c0=c0, war=war, first_ws=first_ws):
            _waits(t, first_ws)
            _waits(t, war)
            mm_fn(t, c0)
        m_mm = B.spe.n

        @B.block.scalar
        def _(s, c0=c0, m_mm=m_mm):
            s.wait_ge(B.spe.h, m_mm)
            if evac_scale is None:
                B.sac.inc(s.activation(evac_dst, B.PSC[:, c0:c0 + 512], AF.Copy))
            else:
                B.sac.inc(s.activation(evac_dst, B.PSC[:, c0:c0 + 512], AF.Copy,
                                       scale=evac_scale))
        B.prr.done(half, B.sac)

    for h in range(16):
        grp, hh = h // 4, h % 4

        @B.block.gpsimd
        def _(g, h=h, grp=grp):
            if h < 2:
                g.wait_ge(spe.h, B.m_p0["pe"])
            else:
                g.wait_ge(spe.h, prod_mm_done[h - 2])
            qg.inc(g.dma_start(out=B.AR2[:, C_WQ[h % 2]:C_WQ[h % 2] + 2048],
                               in_=B.wqkv_d[h, 0, :, :]), 16)
            qg.inc(g.dma_start(out=B.AR2[:, C_WK[h % 2]:C_WK[h % 2] + 2048],
                               in_=B.wqkv_d[h, 1, :, :]), 16)
            if h % 4 == 0:
                if grp >= 1:
                    g.wait_ge(spe.h, grp_v4_mm[grp - 1])
                qg.inc(g.dma_start(out=B.AR2[:, C_WV4:C_WV4 + 8192],
                                   in_=B.wv4_d[grp, :, :]), 16)
        m_qk = qg.n

        first = [(qg, m_qk)]
        if h == 0:
            first += [(qs, m_masks), (sdv, B.m_p0["dv"]), (sac, B.m_p0["ac"])]
        else:
            first += list(B.attn["ctx_evac"])

        if hh == 0:
            for pos in range(16):
                def mkv(pos):
                    def mmf(t, c0, pos=pos):
                        for fc in range(16):
                            mm = t.matmul(
                                B.PSC[:, c0:c0 + 512],
                                xn[:, fc * 2048 + pos * 128:fc * 2048 + (pos + 1) * 128],
                                B.AR2[:, C_WV4 + fc * 512:C_WV4 + (fc + 1) * 512],
                                start=(fc == 0), stop=(fc == 15))
                        B.spe.inc(mm)
                    return mmf
                punit(B, mkv(pos), B.AR2[:, C_V4 + pos * 512:C_V4 + (pos + 1) * 512],
                      None, first)
                first = []
            grp_v4_mm[grp] = B.spe.n

        for tg in range(4):
            def mkk(tg, h=h):
                def mmf(t, c0, tg=tg, h=h):
                    for fc in range(16):
                        mm = t.matmul(
                            B.PSC[:, c0:c0 + 512],
                            B.AR2[:, C_WK[h % 2] + fc * 128:C_WK[h % 2] + (fc + 1) * 128],
                            xn[:, fc * 2048 + tg * 512:fc * 2048 + (tg + 1) * 512],
                            start=(fc == 0), stop=(fc == 15))
                    B.spe.inc(mm)
                return mmf
            punit(B, mkk(tg), B.AR2[:, C_KH + tg * 512:C_KH + (tg + 1) * 512],
                  None, first)
            first = []

        for qg2 in range(2):
            def mkq(qg2, h=h):
                def mmf(t, c0, qg2=qg2, h=h):
                    for fc in range(16):
                        rhs = bass.AP(B.AR1, fc * 2048 + qg2 * 1024,
                                      [[32768, 128], [256, 4], [1, 128]])
                        mm = t.matmul(
                            B.PSC[:, c0:c0 + 512],
                            B.AR2[:, C_WQ[h % 2] + fc * 128:C_WQ[h % 2] + (fc + 1) * 128],
                            rhs, start=(fc == 0), stop=(fc == 15))
                    B.spe.inc(mm)
                return mmf
            punit(B, mkq(qg2), B.AR2[:, C_QH + qg2 * 512:C_QH + (qg2 + 1) * 512],
                  RSQ, first)
            first = []
        prod_mm_done[h] = B.spe.n
        prod_ev_done[h] = B.sac.n

        _attn_slots(
            B,
            qh_ap=lambda i: B.AR2[:, C_QH + i * 128:C_QH + (i + 1) * 128],
            k_ap=lambda c0, n: B.AR2[:, C_KH + c0:C_KH + c0 + n],
            v_ap=lambda ki, hh=hh: B.AR2[:, C_V4 + ki * 512 + hh * 128:
                                         C_V4 + ki * 512 + (hh + 1) * 128],
            ctx_dst=lambda i, h=h: (B.AR3, h * 1024 + i * 128),
            first_wait=[(sac, prod_ev_done[h])],
        )

    wout_ld = [0] * 4
    outproj_mm = [0] * 4
    for of in range(4):
        @B.block.gpsimd
        def _(g, of=of):
            if of == 0:
                g.wait_ge(spe.h, prod_mm_done[15])
            if of >= 2:
                g.wait_ge(spe.h, outproj_mm[of - 2])
            qg.inc(g.dma_start(out=B.AR2[:, (of % 2) * 8192:(of % 2) * 8192 + 8192],
                               in_=B.wout_d[of, :, :]), 16)
        wout_ld[of] = qg.n

        for i in range(NSLOT):
            half, war = B.prr.next()
            c0 = half * 512

            @B.block.tensor
            def _(t, of=of, i=i, c0=c0, war=war,
                  cw=(list(B.attn["ctx_evac"]) if of == 0 and i == 0 else [])):
                t.wait_ge(qg.h, wout_ld[of])
                _waits(t, cw)
                _waits(t, war)
                dst0 = (of % 2) * 8192
                for cf in range(16):
                    mm = t.matmul(B.PSC[:, c0:c0 + 512],
                                  B.AR3[:, cf * 1024 + i * 128:cf * 1024 + (i + 1) * 128],
                                  B.AR2[:, dst0 + cf * 512:dst0 + (cf + 1) * 512],
                                  start=(cf == 0), stop=(cf == 15))
                spe.inc(mm)
            m_mm = spe.n

            sp = (of * 8 + i) % 2

            @B.block.vector
            def _(v, sp=sp, c0=c0, m_mm=m_mm, sw=B._s_stored[sp]):
                v.wait_ge(spe.h, m_mm)
                if sw:
                    v.wait_ge(qs.h, sw)
                sdv.inc(v.tensor_copy(B.sstg[sp][:, :], B.PSC[:, c0:c0 + 512]))
            B.prr.done(half, sdv)
            m_sev = sdv.n

            @B.block.sync
            def _(sy, of=of, i=i, sp=sp, m_sev=m_sev):
                sy.wait_ge(sdv.h, m_sev)
                qs.inc(sy.dma_start(out=B.sdram[i, :, of * 512:(of + 1) * 512],
                                    in_=B.sstg[sp][:, :]), 16)
            B._s_stored[sp] = qs.n
        outproj_mm[of] = spe.n
    B.m_p5_pe = spe.n


def _p1(B):
    qs, qg, spe, sdv, sac = B.qs, B.qg, B.spe, B.sdv, B.sac
    xn = B.AR1
    C_KIN, C_VIN = 16384, 24576
    ar5f = B.AR5.bitcast(F32)

    @B.block.sync
    def _(sy):
        sy.wait_ge(spe.h, B.m_p5_pe)
        qs.inc(sy.dma_start(out=ar5f[:, 0:1024], in_=B.ropec_d[:, :]), 16)
        qs.inc(sy.dma_start(out=ar5f[:, 1024:2048], in_=B.ropes_d[:, :]), 16)
    B._m_rope = qs.n

    @B.block.vector
    def _(v):
        for fc in range(16):
            sdv.inc(v.tensor_scalar_mul(xn[:, fc * 2048:(fc + 1) * 2048],
                                        xn[:, fc * 2048:(fc + 1) * 2048],
                                        B.n1w[:, fc:fc + 1]))
    m_fold = sdv.n

    blk_free_ms = []
    blk_ld = []

    def load_blk(src_ap, idx):
        half = idx % 2

        @B.block.gpsimd
        def _(g, src_ap=src_ap, idx=idx, half=half):
            if idx == 0:
                g.wait_ge(spe.h, B.m_p5_pe)
            if idx >= 2:
                g.wait_ge(spe.h, blk_free_ms[idx - 2])
            qg.inc(g.dma_start(out=B.AR3[:, half * 8192:half * 8192 + 8192],
                               in_=src_ap), 16)
        blk_ld.append(qg.n)
        return half

    def rope_ap(base_t, col0):
        return bass.AP(base_t, col0, [[512, 128], [128, 4], [1, 64]])

    B._rope_qbf_free = 0

    def rope_and_store(dst_base, dst_stride, m_evac, tblcol):
        @B.block.vector
        def _(v, m_evac=m_evac, tblcol=tblcol, qf=B._rope_qbf_free):
            v.wait_ge(sac.h, m_evac)
            v.wait_ge(qs.h, B._m_rope)
            if qf:
                v.wait_ge(spe.h, qf)
            cosv = bass.AP(B.AR5.bitcast(F32), tblcol, [[4096, 128], [0, 4], [1, 64]])
            sinv = bass.AP(B.AR5.bitcast(F32), 1024 + tblcol,
                           [[4096, 128], [0, 4], [1, 64]])
            q1, q2 = rope_ap(B.qsc, 0), rope_ap(B.qsc, 64)
            t1, t2 = rope_ap(B.rsc, 0), rope_ap(B.rsc, 64)
            o1, o2 = rope_ap(B.qbf, 0), rope_ap(B.qbf, 64)
            sdv.inc(v.tensor_tensor(t1, q1, cosv, ALU.mult))
            sdv.inc(v.tensor_tensor(t2, q2, sinv, ALU.mult))
            v.wait_ge(sdv.h, sdv.n)
            sdv.inc(v.tensor_tensor(o1, t1, t2, ALU.subtract))
            sdv.inc(v.tensor_tensor(t1, q2, cosv, ALU.mult))
            sdv.inc(v.tensor_tensor(t2, q1, sinv, ALU.mult))
            v.wait_ge(sdv.h, sdv.n)
            sdv.inc(v.tensor_tensor(o2, t1, t2, ALU.add))
        m_rope_done = sdv.n

        bk, war = B.trr.next()

        @B.block.tensor
        def _(t, bk=bk, war=war, m_rope_done=m_rope_done):
            t.wait_ge(sdv.h, m_rope_done)
            _waits(t, war)
            for j in range(4):
                mm = t.transpose(B.PST[bk][:, j * 128:(j + 1) * 128],
                                 B.qbf[:, j * 128:(j + 1) * 128], B.identb[:, :])
            spe.inc(mm)
        B._rope_qbf_free = spe.n
        m_tr = spe.n

        @B.block.vector
        def _(v, bk=bk, m_tr=m_tr, dst_base=dst_base, dst_stride=dst_stride):
            v.wait_ge(spe.h, m_tr)
            dst = bass.AP(B.AR2, dst_base, [[32768, 128], [dst_stride, 4], [1, 128]])
            sdv.inc(v.tensor_copy(dst, B.PST[bk][:, 0:512]))
        B.trr.done(bk, sdv)
        return sdv.n

    qsc_free = 0

    for of in range(4):
        half = load_blk(B.wq_d[of, :, :], of)
        ld = blk_ld[-1]
        for i in range(NSLOT):
            phalf, war = B.prr.next()
            c0 = phalf * 512

            @B.block.tensor
            def _(t, of=of, i=i, c0=c0, war=war, half=half, ld=ld):
                t.wait_ge(qg.h, ld)
                if of == 0 and i == 0:
                    t.wait_ge(sdv.h, m_fold)
                _waits(t, war)
                for fc in range(16):
                    mm = t.matmul(
                        B.PSC[:, c0:c0 + 512],
                        xn[:, fc * 2048 + (2 * i) * 128:fc * 2048 + (2 * i + 1) * 128],
                        B.AR3[:, half * 8192 + fc * 512:half * 8192 + (fc + 1) * 512],
                        start=(fc == 0), stop=(fc == 15))
                spe.inc(mm)
            m_mm = spe.n

            @B.block.scalar
            def _(s, i=i, c0=c0, m_mm=m_mm, qf=qsc_free):
                s.wait_ge(spe.h, m_mm)
                if qf:
                    s.wait_ge(sdv.h, qf)
                sac.inc(s.activation(B.qsc[:, :], B.PSC[:, c0:c0 + 512], AF.Copy,
                                     scale=B.rstd1q[:, 2 * i:2 * i + 1]))
            B.prr.done(phalf, sac)
            m_ev = sac.n

            qsc_free = rope_and_store(dst_base=(4 * of) * 1024 + i * 128,
                                      dst_stride=1024, m_evac=m_ev,
                                      tblcol=(2 * i) * 64)
        blk_free_ms.append(spe.n)

    halfk = load_blk(B.wk_d[0, :, :], 4)
    ldk = blk_ld[-1]
    for pos in range(16):
        phalf, war = B.prr.next()
        c0 = phalf * 512

        @B.block.tensor
        def _(t, pos=pos, c0=c0, war=war, halfk=halfk, ldk=ldk):
            t.wait_ge(qg.h, ldk)
            _waits(t, war)
            for fc in range(16):
                mm = t.matmul(
                    B.PSC[:, c0:c0 + 512],
                    xn[:, fc * 2048 + pos * 128:fc * 2048 + (pos + 1) * 128],
                    B.AR3[:, halfk * 8192 + fc * 512:halfk * 8192 + (fc + 1) * 512],
                    start=(fc == 0), stop=(fc == 15))
            spe.inc(mm)
        m_mm = spe.n

        @B.block.scalar
        def _(s, pos=pos, c0=c0, m_mm=m_mm, qf=qsc_free):
            s.wait_ge(spe.h, m_mm)
            if qf:
                s.wait_ge(sdv.h, qf)
            sac.inc(s.activation(B.qsc[:, :], B.PSC[:, c0:c0 + 512], AF.Copy,
                                 scale=B.rstd1[:, pos:pos + 1]))
        B.prr.done(phalf, sac)
        m_ev = sac.n

        qsc_free = rope_and_store(dst_base=C_KIN + pos * 128, dst_stride=2048,
                                  m_evac=m_ev, tblcol=pos * 64)
    blk_free_ms.append(spe.n)

    halfv = load_blk(B.wv_d[0, :, :], 5)
    ldv = blk_ld[-1]
    for pos in range(16):
        phalf, war = B.prr.next()
        c0 = phalf * 512

        @B.block.tensor
        def _(t, pos=pos, c0=c0, war=war, halfv=halfv, ldv=ldv):
            t.wait_ge(qg.h, ldv)
            _waits(t, war)
            for fc in range(16):
                mm = t.matmul(
                    B.PSC[:, c0:c0 + 512],
                    xn[:, fc * 2048 + pos * 128:fc * 2048 + (pos + 1) * 128],
                    B.AR3[:, halfv * 8192 + fc * 512:halfv * 8192 + (fc + 1) * 512],
                    start=(fc == 0), stop=(fc == 15))
            spe.inc(mm)
        m_mm = spe.n

        @B.block.scalar
        def _(s, pos=pos, c0=c0, m_mm=m_mm):
            s.wait_ge(spe.h, m_mm)
            sac.inc(s.activation(B.AR2[:, C_VIN + pos * 512:C_VIN + (pos + 1) * 512],
                                 B.PSC[:, c0:c0 + 512], AF.Copy,
                                 scale=B.rstd1[:, pos:pos + 1]))
        B.prr.done(phalf, sac)
    B.m_p1_pe = spe.n


def _p2(B):
    qs, qg, spe, sdv, sac = B.qs, B.qg, B.spe, B.sdv, B.sac
    C_KIN, C_VIN = 16384, 24576

    B._m_qkvtap = 0
    if DEBUG_TAPS:
        @B.block.sync
        def _(sy):
            sy.wait_ge(spe.h, B.m_p1_pe)
            sy.wait_ge(sac.h, B.sac.n)
            sy.wait_ge(sdv.h, B.sdv.n)
            qs.inc(sy.dma_start(out=B.qkv_dbg[:, :], in_=B.AR2[:, :]), 16)
        B._m_qkvtap = qs.n

    for h in range(16):
        kv = h // 4
        _attn_slots(
            B,
            qh_ap=lambda i, h=h: B.AR2[:, h * 1024 + i * 128:h * 1024 + (i + 1) * 128],
            k_ap=lambda c0, n, kv=kv: B.AR2[:, C_KIN + kv * 2048 + c0:
                                            C_KIN + kv * 2048 + c0 + n],
            v_ap=lambda ki, kv=kv: B.AR2[:, C_VIN + ki * 512 + kv * 128:
                                         C_VIN + ki * 512 + (kv + 1) * 128],
            ctx_dst=lambda i, h=h: (B.AR3, h * 1024 + i * 128),
            first_wait=[],
        )
    m_heads = spe.n

    B._m_ctxtap = 0
    if DEBUG_TAPS:
        @B.block.sync
        def _(sy):
            sy.wait_ge(sdv.h, B.sdv.n)
            qs.inc(sy.dma_start(out=B.ctx_dbg[:, :], in_=B.AR3[:, :]), 16)
        B._m_ctxtap = qs.n

        @B.block.tensor
        def _(t):
            t.wait_ge(qs.h, B._m_ctxtap)

    wo_mm = [0] * 4
    xn_used = 0
    hadd = 0
    for of in range(4):
        @B.block.gpsimd
        def _(g, of=of):
            if DEBUG_TAPS and of <= 1:
                g.wait_ge(qs.h, B._m_qkvtap)
            if of <= 1:
                g.wait_ge(spe.h, m_heads)
            else:
                g.wait_ge(spe.h, wo_mm[of - 2])
            dst0 = 8192 + (of % 2) * 8192
            qg.inc(g.dma_start(out=B.AR2[:, dst0:dst0 + 8192], in_=B.wo_d[of, :, :]), 16)
        wo_ld = qg.n

        for i in range(NSLOT):
            bi = i % 2

            @B.block.sync
            def _(sy, of=of, i=i, bi=bi, xu=xn_used):
                if of == 0 and i <= 1:
                    sy.wait_ge(spe.h, m_heads)
                if xu:
                    sy.wait_ge(sdv.h, xu)
                qs.inc(sy.dma_start(out=f32v(B.AR2, bi * 512, 512),
                                    in_=B.xfull[2 * i, :, of * 512:(of + 1) * 512]), 16)
            m_x = qs.n

            half, war = B.prr.next()
            c0 = half * 512

            @B.block.tensor
            def _(t, of=of, i=i, c0=c0, war=war, wo_ld=wo_ld):
                t.wait_ge(qg.h, wo_ld)
                _waits(t, war)
                dst0 = 8192 + (of % 2) * 8192
                for cf in range(16):
                    mm = t.matmul(B.PSC[:, c0:c0 + 512],
                                  B.AR3[:, cf * 1024 + i * 128:cf * 1024 + (i + 1) * 128],
                                  B.AR2[:, dst0 + cf * 512:dst0 + (cf + 1) * 512],
                                  start=(cf == 0), stop=(cf == 15))
                spe.inc(mm)
            m_mm = spe.n

            @B.block.vector
            def _(v, i=i, of=of, bi=bi, c0=c0, m_mm=m_mm, m_x=m_x):
                v.wait_ge(qs.h, m_x)
                sdv.inc(v.tensor_scalar(B.xnsc[:, :], f32v(B.AR2, bi * 512, 512),
                                        B.rstd[:, 2 * i:2 * i + 1],
                                        B.bln[:, 2 * i:2 * i + 1],
                                        ALU.mult, ALU.add))
                v.wait_ge(sdv.h, sdv.n)
                v.wait_ge(spe.h, m_mm)
                hdst = f32v(B.AR1, i * 2048 + of * 512, 512)
                sdv.inc(v.tensor_add(hdst, B.PSC[:, c0:c0 + 512], B.xnsc[:, :]))
            B.prr.done(half, sdv)
            hadd = sdv.n
            xn_used = sdv.n
        wo_mm[of] = spe.n
    B.m_p2_pe = spe.n
    B.m_p2_dv = sdv.n


def _p3(B):
    qs, qg, spe, sdv, sac = B.qs, B.qg, B.spe, B.sdv, B.sac
    hv = lambda i, c0, n: f32v(B.AR1, i * 2048 + c0, n)
    C_U = 0
    C_W1 = [16384, 18432]
    C_W3 = [20480, 22528]
    C_W2 = [24576, 28672]
    sqscr = f32v(B.AR5, 0, 2048)

    @B.block.vector
    def _(v):
        for fc in range(16):
            sdv.inc(v.tensor_scalar_mul(B.gatew[:, fc * 8:(fc + 1) * 8],
                                        B.gatew[:, fc * 8:(fc + 1) * 8],
                                        B.n2w[:, fc:fc + 1]))
    m_gw = sdv.n

    utm_free = [0, 0]
    gev = [0, 0]
    gmm = [0, 0]
    m_sd_prev = 0

    for i in range(NSLOT):
        @B.block.vector
        def _(v, i=i, msp=m_sd_prev):
            if msp:
                v.wait_ge(sac.h, msp)
            sdv.inc(v.scalar_tensor_tensor(sqscr, hv(i, 0, 2048), 1.0, hv(i, 0, 2048),
                                           ALU.mult, ALU.mult, accum_out=B.stat[:, 0:1]))
        m_ss = sdv.n

        @B.block.vector
        def _(v, m_ss=m_ss):
            v.wait_ge(sdv.h, m_ss)
            sdv.inc(v.tensor_scalar(B.stat[:, 1:2], B.stat[:, 0:1], 1.0 / 2048.0, EPS,
                                    ALU.mult, ALU.add))
        m_vps = sdv.n

        @B.block.scalar
        def _(s, m_vps=m_vps):
            s.wait_ge(sdv.h, m_vps)
            sac.inc(s.activation(B.stat[:, 1:2], B.stat[:, 1:2], AF.Sqrt))
        m_sd = sac.n
        m_sd_prev = m_sd

        @B.block.vector
        def _(v, i=i, m_sd=m_sd):
            v.wait_ge(sac.h, m_sd)
            sdv.inc(v.reciprocal(B.rstd2[:, i:i + 1], B.stat[:, 1:2]))
        m_r2 = sdv.n

        @B.block.scalar
        def _(s, i=i, m_r2=m_r2, uf=utm_free[i % 2]):
            s.wait_ge(sdv.h, m_r2)
            if uf:
                s.wait_ge(spe.h, uf)
            uc0 = B.utm_c0[i % 2]
            sac.inc(s.activation(B.AR3[:, uc0:uc0 + 2048], hv(i, 0, 2048), AF.Copy,
                                 scale=B.rstd2[:, i:i + 1]))
        m_u = sac.n

        for g in range(4):
            bk, war = B.trr.next()

            @B.block.tensor
            def _(t, i=i, g=g, bk=bk, war=war, m_u=m_u):
                if g == 0:
                    t.wait_ge(sac.h, m_u)
                _waits(t, war)
                uc0 = B.utm_c0[i % 2]
                for j in range(4):
                    fc = g * 4 + j
                    mm = t.transpose(B.PST[bk][:, j * 128:(j + 1) * 128],
                                     B.AR3[:, uc0 + fc * 128:uc0 + (fc + 1) * 128],
                                     B.identb[:, :])
                spe.inc(mm)
            m_tr = spe.n

            @B.block.vector
            def _(v, i=i, g=g, bk=bk, m_tr=m_tr):
                v.wait_ge(spe.h, m_tr)
                for j in range(4):
                    fc = g * 4 + j
                    sdv.inc(v.tensor_scalar_mul(
                        B.AR2[:, C_U + fc * 1024 + i * 128:C_U + fc * 1024 + (i + 1) * 128],
                        B.PST[bk][:, j * 128:(j + 1) * 128],
                        B.n2w[:, fc:fc + 1]))
            B.trr.done(bk, sdv)
        utm_free[i % 2] = spe.n

        gl_done = 0
        for fc in range(16):
            par = fc % 2
            pc0 = par * 512

            @B.block.tensor
            def _(t, i=i, fc=fc, pc0=pc0, gv=gev[par]):
                if gv:
                    t.wait_ge(sdv.h, gv)
                mm = t.transpose(B.PSC[:, pc0:pc0 + 128], hv(i, fc * 128, 128),
                                 B.identf[:, :])
                spe.inc(mm)
            m_gt = spe.n

            @B.block.vector
            def _(v, par=par, pc0=pc0, m_gt=m_gt, gm=gmm[par]):
                v.wait_ge(spe.h, m_gt)
                if gm:
                    v.wait_ge(spe.h, gm)
                sdv.inc(v.tensor_copy(B.gfm[par][:, :], B.PSC[:, pc0:pc0 + 128]))
            gev[par] = sdv.n
            m_gev = sdv.n

            @B.block.tensor
            def _(t, fc=fc, par=par, m_gev=m_gev):
                t.wait_ge(sdv.h, m_gev)
                if fc == 0:
                    t.wait_ge(sdv.h, m_gw)
                mm = t.matmul(B.PSB[:, 0:8], B.gfm[par][:, :],
                              B.gatew[:, fc * 8:(fc + 1) * 8],
                              start=(fc == 0), stop=(fc == 15), skip_group_check=True)
                if fc == 15:
                    spe.inc(mm)
            gmm[par] = spe.n
        gl_done = spe.n

        @B.block.vector
        def _(v, i=i, gl_done=gl_done):
            v.wait_ge(spe.h, gl_done)
            sdv.inc(v.tensor_scalar_mul(B.logits[:, i * 8:(i + 1) * 8], B.PSB[:, 0:8],
                                        B.rstd2[:, i:i + 1]))
    m_logits = sdv.n

    for i in range(NSLOT):
        @B.block.vector
        def _(v, i=i):
            lg = B.logits[:, i * 8:(i + 1) * 8]
            if i == 0:
                v.wait_ge(sdv.h, m_logits)
            sdv.inc(v.reduce_max(B.v1[:, :], lg, axis=AX.X))
            v.wait_ge(sdv.h, sdv.n)
            sdv.inc(v.tensor_scalar(B.m1[:, :], lg, B.v1[:, 0:1], None, ALU.is_equal))
            v.wait_ge(sdv.h, sdv.n)
            sdv.inc(v.scalar_tensor_tensor(B.lgx[:, :], B.m1[:, :], -1e30, lg,
                                           ALU.mult, ALU.add))
            v.wait_ge(sdv.h, sdv.n)
            sdv.inc(v.reduce_max(B.v2[:, :], B.lgx[:, :], axis=AX.X))
            v.wait_ge(sdv.h, sdv.n)
            sdv.inc(v.tensor_scalar(B.m2[:, :], B.lgx[:, :], B.v2[:, 0:1], None,
                                    ALU.is_equal))
            sdv.inc(v.tensor_sub(B.p2c[:, :], B.v2[:, :], B.v1[:, :]))
        m_d = sdv.n

        @B.block.scalar
        def _(s, m_d=m_d):
            s.wait_ge(sdv.h, m_d)
            sac.inc(s.activation(B.p2c[:, :], B.p2c[:, :], AF.Sigmoid))
        m_sig = sac.n

        @B.block.vector
        def _(v, i=i, m_sig=m_sig):
            v.wait_ge(sac.h, m_sig)
            sdv.inc(v.tensor_scalar(B.p1c[:, :], B.p2c[:, :], -1.0, 1.0,
                                    ALU.mult, ALU.add))
            v.wait_ge(sdv.h, sdv.n)
            sdv.inc(v.tensor_scalar_mul(B.wte[:, i * 8:(i + 1) * 8], B.m1[:, :],
                                        B.p1c[:, 0:1]))
            v.wait_ge(sdv.h, sdv.n)
            sdv.inc(v.scalar_tensor_tensor(B.wte[:, i * 8:(i + 1) * 8], B.m2[:, :],
                                           B.p2c[:, 0:1], B.wte[:, i * 8:(i + 1) * 8],
                                           ALU.mult, ALU.add))
    m_wte = sdv.n

    B._m_hdbg = 0
    if DEBUG_TAPS:
        @B.block.sync
        def _(sy):
            sy.wait_ge(sdv.h, B.m_p2_dv)
            for i in range(NSLOT):
                qs.inc(sy.dma_start(out=B.h_dbg[i, :, :],
                                    in_=f32v(B.AR1, i * 2048, 2048)), 16)
        B._m_hdbg = qs.n

    silu = [f32v(B.AR5, 0, 512), f32v(B.AR5, 512, 512)]
    abrot = Rot(2)
    bbrot = Rot(2)
    w1_free = {}
    w2_free = {}
    he_free = [[], []]

    for e in range(NE):
        for qtr in range(4):
            hbuf = qtr % 2
            for ci in range(8):
                c128 = qtr * 8 + ci
                w1h = c128 % 2

                @B.block.gpsimd
                def _(g, e=e, c128=c128, w1h=w1h):
                    if e == 0 and c128 == 0:
                        g.wait_ge(spe.h, B.m_p2_pe)
                    prev = (e, c128 - 2) if c128 >= 2 else (e - 1, 30 + c128)
                    if prev in w1_free:
                        g.wait_ge(spe.h, w1_free[prev])
                    qg.inc(g.dma_start(out=B.AR2[:, C_W1[w1h]:C_W1[w1h] + 2048],
                                       in_=B.w1_d[e, c128, :, :]), 16)
                    qg.inc(g.dma_start(out=B.AR2[:, C_W3[w1h]:C_W3[w1h] + 2048],
                                       in_=B.w3_d[e, c128, :, :]), 16)
                m_wld = qg.n

                for th in range(2):
                    abk, awar = abrot.next()
                    bbk, bwar = bbrot.next()
                    ac0 = abk * 512
                    bc0 = 1024 + bbk * 512

                    @B.block.tensor
                    def _(t, e=e, qtr=qtr, ci=ci, th=th, w1h=w1h, ac0=ac0, bc0=bc0,
                          awar=awar, bwar=bwar, m_wld=m_wld, hbuf=hbuf):
                        t.wait_ge(qg.h, m_wld)
                        if ci == 0 and th == 0:
                            _waits(t, he_free[hbuf])
                            if e == 0 and qtr == 0:
                                t.wait_ge(sdv.h, m_wte)
                        _waits(t, awar)
                        for fc in range(16):
                            mm = t.matmul(
                                B.PSB[:, ac0:ac0 + 512],
                                B.AR2[:, C_W1[w1h] + fc * 128:C_W1[w1h] + (fc + 1) * 128],
                                B.AR2[:, C_U + fc * 1024 + th * 512:
                                      C_U + fc * 1024 + (th + 1) * 512],
                                start=(fc == 0), stop=(fc == 15))
                        spe.inc(mm)
                        _waits(t, bwar)
                        for fc in range(16):
                            mm = t.matmul(
                                B.PSB[:, bc0:bc0 + 512],
                                B.AR2[:, C_W3[w1h] + fc * 128:C_W3[w1h] + (fc + 1) * 128],
                                B.AR2[:, C_U + fc * 1024 + th * 512:
                                      C_U + fc * 1024 + (th + 1) * 512],
                                start=(fc == 0), stop=(fc == 15))
                        spe.inc(mm)
                    m_ab = spe.n

                    @B.block.scalar
                    def _(s, th=th, ac0=ac0, m_ab=m_ab):
                        s.wait_ge(spe.h, m_ab)
                        sac.inc(s.activation(silu[th], B.PSB[:, ac0:ac0 + 512], AF.Silu))
                    abrot.done(abk, sac)
                    m_si = sac.n

                    @B.block.vector
                    def _(v, ci=ci, th=th, bc0=bc0, hbuf=hbuf, m_si=m_si):
                        v.wait_ge(sac.h, m_si)
                        dst = B.AR3[:, hbuf * 8192 + ci * 1024 + th * 512:
                                    hbuf * 8192 + ci * 1024 + (th + 1) * 512]
                        sdv.inc(v.tensor_mul(dst, silu[th], B.PSB[:, bc0:bc0 + 512]))
                    bbrot.done(bbk, sdv)
                w1_free[(e, c128)] = spe.n
            m_he = sdv.n

            for of in range(4):
                w2h = of % 2

                @B.block.gpsimd
                def _(g, e=e, qtr=qtr, of=of, w2h=w2h):
                    if of >= 2:
                        prev = (e, qtr, of - 2)
                    elif qtr >= 1:
                        prev = (e, qtr - 1, 2 + of)
                    elif e >= 1:
                        prev = (e - 1, 3, 2 + of)
                    else:
                        prev = None
                    if prev in w2_free:
                        g.wait_ge(spe.h, w2_free[prev])
                    qg.inc(g.dma_start(out=B.AR2[:, C_W2[w2h]:C_W2[w2h] + 4096],
                                       in_=B.w2_d[e, qtr, of, :, :]), 16)
                m_w2 = qg.n

                for i in range(NSLOT):
                    half, war = B.prr.next()
                    c0 = half * 512

                    @B.block.tensor
                    def _(t, of=of, i=i, c0=c0, war=war, w2h=w2h, hbuf=hbuf,
                          m_w2=m_w2, m_he=m_he):
                        t.wait_ge(qg.h, m_w2)
                        if of == 0 and i == 0:
                            t.wait_ge(sdv.h, m_he)
                        _waits(t, war)
                        for ci in range(8):
                            mm = t.matmul(
                                B.PSC[:, c0:c0 + 512],
                                B.AR3[:, hbuf * 8192 + ci * 1024 + i * 128:
                                      hbuf * 8192 + ci * 1024 + (i + 1) * 128],
                                B.AR2[:, C_W2[w2h] + ci * 512:C_W2[w2h] + (ci + 1) * 512],
                                start=(ci == 0), stop=(ci == 7))
                        spe.inc(mm)
                    m_mm = spe.n

                    @B.block.vector
                    def _(v, e=e, qtr=qtr, i=i, of=of, c0=c0, m_mm=m_mm):
                        v.wait_ge(spe.h, m_mm)
                        if DEBUG_TAPS and e == 0 and qtr == 0 and of == 0 and i == 0:
                            v.wait_ge(qs.h, B._m_hdbg)
                        hdst = f32v(B.AR1, i * 2048 + of * 512, 512)
                        sdv.inc(v.scalar_tensor_tensor(
                            hdst, B.PSC[:, c0:c0 + 512],
                            B.wte[:, i * 8 + e:i * 8 + e + 1],
                            hdst, ALU.mult, ALU.add))
                    B.prr.done(half, sdv)
                w2_free[(e, qtr, of)] = spe.n
            he_free[hbuf] = [(spe, spe.n)]
    B.m_p3_pe = spe.n


def _p4(B):
    qs, spe, sdv, sac = B.qs, B.spe, B.sdv, B.sac
    used = [0] * 8
    stored = [0] * 8
    for i in range(NSLOT):
        bi = i % 2
        xt = f32v(B.AR2, bi * 2048, 2048)
        gt = f32v(B.AR2, 4096 + bi * 2048, 2048)
        ot = f32v(B.AR2, 8192 + bi * 2048, 2048)

        @B.block.sync
        def _(sy, i=i, bi=bi, xt=xt):
            if i < 2:
                sy.wait_ge(spe.h, B.m_p3_pe)
                sy.wait_ge(qs.h, B._s_stored[0])
                sy.wait_ge(qs.h, B._s_stored[1])
            else:
                sy.wait_ge(sdv.h, used[i - 2])
            qs.inc(sy.dma_start(out=xt, in_=B.xfull[2 * i, :, :]), 16)
            qs.inc(sy.dma_start(out=B.sld[bi][:, :], in_=B.sdram[i, :, :]), 16)
        m_x = qs.n

        @B.block.scalar
        def _(s, i=i, gt=gt):
            if i >= 2:
                s.wait_ge(sdv.h, used[i - 2])
            sac.inc(s.activation(gt, f32v(B.AR1, i * 2048, 2048), AF.Silu))
        m_g = sac.n

        @B.block.vector
        def _(v, i=i, bi=bi, xt=xt, gt=gt, ot=ot, m_x=m_x, m_g=m_g):
            v.wait_ge(sac.h, m_g)
            v.wait_ge(qs.h, m_x)
            if i >= 2:
                v.wait_ge(qs.h, stored[i - 2])
            sdv.inc(v.tensor_mul(ot, B.sld[bi][:, :], gt))
            v.wait_ge(sdv.h, sdv.n)
            sdv.inc(v.tensor_add(ot, ot, xt))
        used[i] = sdv.n
        m_out = sdv.n

        @B.block.sync
        def _(sy, i=i, bi=bi, ot=ot, m_out=m_out, m_g=m_g, m_x=m_x):
            sy.wait_ge(sdv.h, m_out)
            qs.inc(sy.dma_start(out=B.yout[i, :, :], in_=ot), 16)
            if DEBUG_TAPS:
                qs.inc(sy.dma_start(out=B.s_dbg[i, :, :], in_=B.sld[bi][:, :]), 16)
                qs.inc(sy.dma_start(out=B.f_dbg[i, :, :],
                                    in_=f32v(B.AR1, i * 2048, 2048)), 16)
        stored[i] = qs.n

    final = qs.n

    @B.block.sync
    def _(sy):
        sy.wait_ge(qs.h, final)



_NC_CACHE = {}


def _get_nc():
    if "nc" not in _NC_CACHE:
        _NC_CACHE["nc"] = build_program()
    return _NC_CACHE["nc"]


def _img_fm(w, cols):
    Dd, C = w.shape
    nb = C // cols
    wr = w.reshape(16, 128, nb, cols)
    out = np.empty((nb, 128, 16 * cols), dtype=np.float32)
    for blk in range(nb):
        out[blk] = wr[:, :, blk, :].transpose(1, 0, 2).reshape(128, 16 * cols)
    return np.ascontiguousarray(out)


def kernel(L0_attn_x, L0_attn_w_qkv, L0_attn_w_out, L0_ffn_norm1_w, L0_ffn_norm2_w,
           L0_ffn_wq, L0_ffn_wk, L0_ffn_wv, L0_ffn_wo, L0_ffn_gate_w,
           L0_ffn_w1, L0_ffn_w3, L0_ffn_w2):
    x = np.ascontiguousarray(np.asarray(L0_attn_x, dtype=np.float32))
    wqkv = np.asarray(L0_attn_w_qkv, dtype=np.float32)
    wout = np.asarray(L0_attn_w_out, dtype=np.float32)
    n1w = np.asarray(L0_ffn_norm1_w, dtype=np.float32)
    n2w = np.asarray(L0_ffn_norm2_w, dtype=np.float32)
    wq = np.asarray(L0_ffn_wq, dtype=np.float32)
    wk = np.asarray(L0_ffn_wk, dtype=np.float32)
    wv = np.asarray(L0_ffn_wv, dtype=np.float32)
    wo = np.asarray(L0_ffn_wo, dtype=np.float32)
    gw = np.asarray(L0_ffn_gate_w, dtype=np.float32)
    w1 = np.asarray(L0_ffn_w1, dtype=np.float32)
    w3 = np.asarray(L0_ffn_w3, dtype=np.float32)
    w2 = np.asarray(L0_ffn_w2, dtype=np.float32)

    wqkv_img = np.empty((16, 3, 128, 2048), dtype=np.float32)
    for h in range(16):
        for m in range(3):
            cols = wqkv[:, m * 2048 + h * 128:m * 2048 + (h + 1) * 128]
            wqkv_img[h, m] = cols.reshape(16, 128, 128).transpose(1, 0, 2).reshape(128, 2048)
    wv4_img = np.empty((4, 128, 8192), dtype=np.float32)
    for g in range(4):
        cols = wqkv[:, 2 * 2048 + g * 512:2 * 2048 + (g + 1) * 512]
        wv4_img[g] = cols.reshape(16, 128, 512).transpose(1, 0, 2).reshape(128, 8192)
    wout_img = _img_fm(wout, 512)
    wq_img = _img_fm(wq, 512)
    wk_img = _img_fm(wk, 512)
    wv_img = _img_fm(wv, 512)
    wo_img = _img_fm(wo, 512)
    w1_img = np.empty((8, 32, 128, 2048), dtype=np.float32)
    w3_img = np.empty((8, 32, 128, 2048), dtype=np.float32)
    for e in range(8):
        w1r = w1[e].reshape(16, 128, 32, 128)
        w3r = w3[e].reshape(16, 128, 32, 128)
        for c in range(32):
            w1_img[e, c] = w1r[:, :, c, :].transpose(1, 0, 2).reshape(128, 2048)
            w3_img[e, c] = w3r[:, :, c, :].transpose(1, 0, 2).reshape(128, 2048)
    w2_img = np.empty((8, 4, 4, 128, 4096), dtype=np.float32)
    w2r = w2.reshape(8, 32, 128, 4, 512)
    for e in range(8):
        for q in range(4):
            for of in range(4):
                blk = w2r[e, q * 8:(q + 1) * 8, :, of, :]
                w2_img[e, q, of] = blk.transpose(1, 0, 2).reshape(128, 4096)
    wqkv_img = np.ascontiguousarray(wqkv_img)
    wv4_img = np.ascontiguousarray(wv4_img)
    w1_img = np.ascontiguousarray(w1_img)
    w3_img = np.ascontiguousarray(w3_img)
    w2_img = np.ascontiguousarray(w2_img)

    n1w_img = np.ascontiguousarray(n1w.reshape(16, 128).T)
    n2w_img = np.ascontiguousarray(n2w.reshape(16, 128).T)
    gw_img = np.ascontiguousarray(
        gw.reshape(16, 128, 8).transpose(1, 0, 2).reshape(128, 128))
    identb = np.eye(128, dtype=ml_dtypes.bfloat16)
    identf = np.eye(128, dtype=np.float32)

    pos = np.arange(2048, dtype=np.float64)
    inv = 1.0 / (1e6 ** (np.arange(0, 128, 2, dtype=np.float64) / 128.0))
    ang = pos[:, None] * inv[None, :]
    cosf = np.cos(ang).astype(np.float32)
    sinf = np.sin(ang).astype(np.float32)

    nc = _get_nc()
    in_maps = []
    for c in range(8):
        b, par = c // 2, c % 2
        perm = np.arange(16) ^ par
        xf = np.ascontiguousarray(x[b].reshape(16, 128, 2048)[perm])
        ropec = np.ascontiguousarray(
            cosf.reshape(16, 128, 64)[perm].transpose(1, 0, 2).reshape(128, 1024))
        ropes = np.ascontiguousarray(
            sinf.reshape(16, 128, 64)[perm].transpose(1, 0, 2).reshape(128, 1024))
        masks = np.zeros((128, 2048), dtype=np.float32)
        tri = np.triu(np.full((128, 128), NEG, dtype=np.float32), 1)
        for i in range(8):
            masks[:, i * 256:i * 256 + 128] = tri
            if par == 0:
                masks[:, i * 256 + 128:i * 256 + 256] = NEG
        masks = masks.astype(ml_dtypes.bfloat16)
        in_maps.append({
            "xfull": xf,
            "identb_d": identb,
            "identf_d": identf,
            "masks_d": masks,
            "ropec_d": ropec,
            "ropes_d": ropes,
            "n1w_d": n1w_img,
            "n2w_d": n2w_img,
            "gatew_d": gw_img,
            "wqkv_d": wqkv_img,
            "wv4_d": wv4_img,
            "wout_d": wout_img,
            "wq_d": wq_img,
            "wk_d": wk_img,
            "wv_d": wv_img,
            "wo_d": wo_img,
            "w1_d": w1_img,
            "w3_d": w3_img,
            "w2_d": w2_img,
        })

    res = run_bass_kernel_spmd(nc, in_maps, core_ids=list(range(8)))

    out = np.empty((4, 2048, 2048), dtype=np.float32)
    for c in range(8):
        b, par = c // 2, c % 2
        y = res.results[c]["yout"]
        for i in range(8):
            g = 2 * i + par
            out[b, g * 128:(g + 1) * 128, :] = y[i]
    if DEBUG_TAPS:
        DEBUG_RES["s"] = np.empty((4, 2048, 2048), dtype=np.float32)
        DEBUG_RES["f"] = np.empty((4, 2048, 2048), dtype=np.float32)
        for c in range(8):
            b, par = c // 2, c % 2
            for i in range(8):
                g = 2 * i + par
                DEBUG_RES["s"][b, g * 128:(g + 1) * 128, :] = \
                    res.results[c]["s_dbg"][i].astype(np.float32)
                DEBUG_RES["f"][b, g * 128:(g + 1) * 128, :] = res.results[c]["f_dbg"][i]
    if DEBUG_TAPS:
        DEBUG_RES["h"] = np.empty((4, 2048, 2048), dtype=np.float32)
        for c in range(8):
            b, par = c // 2, c % 2
            for i in range(8):
                g = 2 * i + par
                DEBUG_RES["h"][b, g * 128:(g + 1) * 128, :] = res.results[c]["h_dbg"][i]
        DEBUG_RES["qkv"] = [res.results[c]["qkv_dbg"].astype(np.float32) for c in range(8)]
        DEBUG_RES["ctx"] = [res.results[c]["ctx_dbg"].astype(np.float32) for c in range(8)]
    return out


# version 04
# speedup vs baseline: 1.0083x; 1.0083x over previous
import numpy as np
import ml_dtypes
from contextlib import ExitStack

import concourse.bass as bass
import concourse.mybir as mybir
from concourse.bass_utils import run_bass_kernel_spmd

F32 = mybir.dt.float32
BF16 = mybir.dt.bfloat16
AF = mybir.ActivationFunctionType
ALU = mybir.AluOpType
AX = mybir.AxisListType

NSLOT = 8
NE = 8
DEBUG_TAPS = False
DEBUG_RES = {}
EPS = 1e-5
RSQ = float(1.0 / np.sqrt(128.0))
NEG = -1e9


class Sem:
    def __init__(self, h):
        self.h = h
        self.n = 0

    def inc(self, instr, k=1):
        instr.then_inc(self.h, k)
        self.n += k
        return self.n


class Rot:
    def __init__(self, n):
        self.n = n
        self.last = [[] for _ in range(n)]
        self.cnt = 0

    def next(self):
        i = self.cnt % self.n
        self.cnt += 1
        return i, list(self.last[i])

    def done(self, i, sem):
        self.last[i] = [(sem, sem.n)]


class Bld:
    pass


def f32v(t, c0, n):
    return t.bitcast(F32)[:, c0:c0 + n]


def _waits(eng, ws):
    for sem, val in ws:
        if val:
            eng.wait_ge(sem.h, val)


def build_program():
    nc = bass.Bass("TRN2", target_bir_lowering=False)
    B = Bld()
    B.nc = nc

    di = lambda name, shape, dt: nc.dram_tensor(name, shape, dt, kind="ExternalInput")
    B.xfull = di("xfull", [16, 128, 2048], F32)
    B.identb_d = di("identb_d", [128, 128], BF16)
    B.identf_d = di("identf_d", [128, 128], F32)
    B.masks_d = di("masks_d", [128, 2048], BF16)
    B.ropec_d = di("ropec_d", [128, 1024], F32)
    B.ropes_d = di("ropes_d", [128, 1024], F32)
    B.n1w_d = di("n1w_d", [128, 16], F32)
    B.n2w_d = di("n2w_d", [128, 16], F32)
    B.gatew_d = di("gatew_d", [128, 128], F32)
    B.wqkv_d = di("wqkv_d", [16, 3, 128, 2048], F32)
    B.wv4_d = di("wv4_d", [4, 128, 8192], F32)
    B.wout_d = di("wout_d", [4, 128, 8192], F32)
    B.wq_d = di("wq_d", [4, 128, 8192], F32)
    B.wk_d = di("wk_d", [1, 128, 8192], F32)
    B.wv_d = di("wv_d", [1, 128, 8192], F32)
    B.wo_d = di("wo_d", [4, 128, 8192], F32)
    B.w1_d = di("w1_d", [8, 32, 128, 2048], F32)
    B.w3_d = di("w3_d", [8, 32, 128, 2048], F32)
    B.w2_d = di("w2_d", [8, 4, 4, 128, 4096], F32)
    B.yout = nc.dram_tensor("yout", [8, 128, 2048], F32, kind="ExternalOutput")
    B.sdram = nc.dram_tensor("sdram", [8, 128, 2048], BF16, kind="Internal")
    if DEBUG_TAPS:
        B.s_dbg = nc.dram_tensor("s_dbg", [8, 128, 2048], BF16, kind="ExternalOutput")
        B.h_dbg = nc.dram_tensor("h_dbg", [8, 128, 2048], F32, kind="ExternalOutput")
        B.qkv_dbg = nc.dram_tensor("qkv_dbg", [128, 32768], BF16, kind="ExternalOutput")
        B.ctx_dbg = nc.dram_tensor("ctx_dbg", [128, 16384], BF16, kind="ExternalOutput")
        B.f_dbg = nc.dram_tensor("f_dbg", [8, 128, 2048], F32, kind="ExternalOutput")

    ctx = ExitStack()
    sb = lambda name, shape, dt: ctx.enter_context(nc.sbuf_tensor(name, shape, dt))
    ps = lambda name, shape, dt: ctx.enter_context(nc.psum_tensor(name, shape, dt))

    B.AR1 = sb("AR1", [128, 32768], BF16)
    B.AR2 = sb("AR2", [128, 32768], BF16)
    B.AR3 = sb("AR3", [128, 16384], BF16)
    B.AR5 = sb("AR5", [128, 8192], BF16)

    B.identb = sb("identb", [128, 128], BF16)
    B.identf = sb("identf", [128, 128], F32)
    B.n1w = sb("n1w", [128, 16], F32)
    B.n2w = sb("n2w", [128, 16], F32)
    B.gatew = sb("gatew", [128, 128], F32)
    B.rstd = sb("rstd", [128, 16], F32)
    B.bln = sb("bln", [128, 16], F32)
    B.rstd1 = sb("rstd1", [128, 16], F32)
    B.rstd1q = sb("rstd1q", [128, 16], F32)
    B.rstd2 = sb("rstd2", [128, 8], F32)
    B.stat = sb("stat", [128, 8], F32)
    B.negm = sb("negm", [128, 2], F32)
    B.rsum = sb("rsum", [128, 2], F32)
    B.rrec = sb("rrec", [128, 2], F32)
    B.logits = sb("logits", [128, 64], F32)
    B.wte = sb("wte", [128, 64], F32)
    B.v1 = sb("v1", [128, 1], F32)
    B.v2 = sb("v2", [128, 1], F32)
    B.p1c = sb("p1c", [128, 1], F32)
    B.p2c = sb("p2c", [128, 1], F32)
    B.m1 = sb("m1", [128, 8], F32)
    B.m2 = sb("m2", [128, 8], F32)
    B.lgx = sb("lgx", [128, 8], F32)
    B.xnbf_c0 = [0, 2048]
    B.qsc = [sb("qsc0", [128, 512], F32), sb("qsc1", [128, 512], F32)]
    B.rsc = sb("rsc", [128, 512], F32)
    B.qbf = [sb("qbf0", [128, 512], BF16), sb("qbf1", [128, 512], BF16)]
    B.utm_c0 = [0, 2048]
    B.xnsc = sb("xnsc", [128, 512], F32)
    B.sstg = [sb("sstg0", [128, 512], BF16), sb("sstg1", [128, 512], BF16)]
    B.sld = [sb("sld0", [128, 2048], BF16), sb("sld1", [128, 2048], BF16)]
    B.gfm = [sb("gfm0", [128, 128], F32), sb("gfm1", [128, 128], F32)]

    B.PSB = ps("PSB", [128, 2048], F32)
    B.PSC = ps("PSC", [128, 1024], F32)
    B.PST = [ps("PST0", [128, 1024], BF16), ps("PST1", [128, 1024], BF16)]

    B.spe = Sem(ctx.enter_context(nc.semaphore("spe")))
    B.sdv = Sem(ctx.enter_context(nc.semaphore("sdv")))
    B.sac = Sem(ctx.enter_context(nc.semaphore("sac")))
    B.qs = Sem(ctx.enter_context(nc.semaphore("qs")))
    B.qg = Sem(ctx.enter_context(nc.semaphore("qg")))

    B.trr = Rot(2)
    B.prr = Rot(2)
    B.attn = {"exp_done": 0, "ctx_evac": [], "pt_free": 0,
              "exp_par_0": 0, "exp_par_1": 0, "ctmp_free": [0, 0]}
    B._ctmp_cnt = 0

    B.block = ctx.enter_context(nc.Block())

    _consts(B)
    _p0(B)
    _p5(B)
    _fence(B)
    _p1(B)
    _fence(B)
    _p2(B)
    _fence(B)
    _p3(B)
    _fence(B)
    _p4(B)

    ctx.close()
    return nc


def _fence(B):
    pe_n, dv_n, ac_n = B.spe.n, B.sdv.n, B.sac.n

    @B.block.tensor
    def _(t):
        t.wait_ge(B.sdv.h, dv_n)
        t.wait_ge(B.sac.h, ac_n)

    @B.block.vector
    def _(v):
        v.wait_ge(B.spe.h, pe_n)
        v.wait_ge(B.sac.h, ac_n)

    @B.block.scalar
    def _(s):
        s.wait_ge(B.spe.h, pe_n)
        s.wait_ge(B.sdv.h, dv_n)


def _consts(B):
    qs = B.qs

    @B.block.sync
    def _(sy):
        qs.inc(sy.dma_start(out=B.identb[:, :], in_=B.identb_d[:, :]), 16)
        qs.inc(sy.dma_start(out=B.identf[:, :], in_=B.identf_d[:, :]), 16)
        qs.inc(sy.dma_start(out=B.n1w[:, :], in_=B.n1w_d[:, :]), 16)
        qs.inc(sy.dma_start(out=B.n2w[:, :], in_=B.n2w_d[:, :]), 16)
        qs.inc(sy.dma_start(out=B.gatew[:, :], in_=B.gatew_d[:, :]), 16)
    B.m_consts = qs.n


def _p0(B):
    qs, sdv, sac, spe = B.qs, B.sdv, B.sac, B.spe
    xt = [f32v(B.AR3, 0, 2048), f32v(B.AR3, 2048, 2048)]
    xm = f32v(B.AR5, 0, 2048)
    sq = f32v(B.AR5, 2048, 2048)

    actdone = [0] * 16
    xnbf_free = [0] * 16

    for p in range(16):
        buf = p % 2

        @B.block.sync
        def _(sy, p=p, buf=buf):
            if p == 0:
                sy.wait_ge(qs.h, B.m_consts)
            if p >= 2:
                sy.wait_ge(sdv.h, actdone[p - 2])
            qs.inc(sy.dma_start(out=xt[buf], in_=B.xfull[p, :, :]), 16)
        m_load = qs.n

        @B.block.vector
        def _(v, p=p, buf=buf, m_load=m_load):
            v.wait_ge(qs.h, m_load)
            sdv.inc(v.reduce_sum(B.stat[:, 0:1], xt[buf], axis=AX.X))
            v.wait_ge(sdv.h, sdv.n)
            sdv.inc(v.tensor_scalar_mul(B.stat[:, 1:2], B.stat[:, 0:1], -1.0 / 2048.0))
        m_negm = sdv.n

        @B.block.vector
        def _(v, buf=buf, m_negm=m_negm):
            v.wait_ge(sdv.h, m_negm)
            sdv.inc(v.tensor_scalar(xm, xt[buf], B.stat[:, 1:2], None, ALU.add))
            v.wait_ge(sdv.h, sdv.n)
            sdv.inc(v.scalar_tensor_tensor(sq, xm, 1.0, xm, ALU.mult, ALU.mult,
                                           accum_out=B.stat[:, 2:3]))
        m_ss = sdv.n

        @B.block.vector
        def _(v, m_ss=m_ss):
            v.wait_ge(sdv.h, m_ss)
            sdv.inc(v.tensor_scalar(B.stat[:, 3:4], B.stat[:, 2:3], 1.0 / 2048.0, EPS,
                                    ALU.mult, ALU.add))
        m_vps = sdv.n

        @B.block.scalar
        def _(s, m_vps=m_vps):
            s.wait_ge(sdv.h, m_vps)
            sac.inc(s.activation(B.stat[:, 3:4], B.stat[:, 3:4], AF.Sqrt))
        m_sd = sac.n

        @B.block.vector
        def _(v, p=p, m_sd=m_sd):
            v.wait_ge(sac.h, m_sd)
            sdv.inc(v.reciprocal(B.rstd[:, p:p + 1], B.stat[:, 3:4]))
            v.wait_ge(sdv.h, sdv.n)
            sdv.inc(v.tensor_mul(B.bln[:, p:p + 1], B.stat[:, 1:2], B.rstd[:, p:p + 1]))
            sdv.inc(v.tensor_mul(B.stat[:, 4:5], B.rstd[:, p:p + 1], B.rstd[:, p:p + 1]))
            v.wait_ge(sdv.h, sdv.n)
            sdv.inc(v.tensor_scalar_mul(B.stat[:, 5:6], B.stat[:, 2:3], 1.0 / 2048.0))
            v.wait_ge(sdv.h, sdv.n)
            sdv.inc(v.tensor_mul(B.stat[:, 6:7], B.stat[:, 5:6], B.stat[:, 4:5]))
        m_vr = sdv.n

        @B.block.vector
        def _(v, m_vr=m_vr):
            v.wait_ge(sdv.h, m_vr)
            sdv.inc(v.tensor_scalar(B.stat[:, 7:8], B.stat[:, 6:7], EPS, None, ALU.add))
        m_vr2 = sdv.n

        @B.block.scalar
        def _(s, m_vr2=m_vr2):
            s.wait_ge(sdv.h, m_vr2)
            sac.inc(s.activation(B.stat[:, 7:8], B.stat[:, 7:8], AF.Sqrt))
        m_sd1 = sac.n

        @B.block.vector
        def _(v, p=p, m_sd1=m_sd1):
            v.wait_ge(sac.h, m_sd1)
            sdv.inc(v.reciprocal(B.rstd1[:, p:p + 1], B.stat[:, 7:8]))
            v.wait_ge(sdv.h, sdv.n)
            sdv.inc(v.tensor_scalar_mul(B.rstd1q[:, p:p + 1], B.rstd1[:, p:p + 1], RSQ))
        m_r1 = sdv.n

        @B.block.vector
        def _(v, p=p, buf=buf, m_r1=m_r1):
            v.wait_ge(sdv.h, m_r1)
            if p >= 2:
                v.wait_ge(spe.h, xnbf_free[p - 2])
            c0 = B.xnbf_c0[buf]
            sdv.inc(v.tensor_scalar(B.AR2[:, c0:c0 + 2048], xt[buf],
                                    B.rstd[:, p:p + 1], B.bln[:, p:p + 1],
                                    ALU.mult, ALU.add))
        actdone[p] = sdv.n
        m_xn = sdv.n

        for g in range(4):
            bk, war = B.trr.next()

            @B.block.tensor
            def _(t, buf=buf, g=g, bk=bk, war=war, m_xn=m_xn):
                if g == 0:
                    t.wait_ge(sdv.h, m_xn)
                _waits(t, war)
                c0 = B.xnbf_c0[buf]
                for j in range(4):
                    fc = g * 4 + j
                    mm = t.transpose(B.PST[bk][:, j * 128:(j + 1) * 128],
                                     B.AR2[:, c0 + fc * 128:c0 + (fc + 1) * 128],
                                     B.identb[:, :])
                spe.inc(mm)
            m_tr = spe.n

            @B.block.vector
            def _(v, p=p, g=g, bk=bk, m_tr=m_tr):
                v.wait_ge(spe.h, m_tr)
                dst = bass.AP(B.AR1, (g * 4) * 2048 + p * 128,
                              [[32768, 128], [2048, 4], [1, 128]])
                sdv.inc(v.tensor_copy(dst, B.PST[bk][:, 0:512]))
            B.trr.done(bk, sdv)
        xnbf_free[p] = spe.n

    B.m_p0 = {"pe": spe.n, "dv": sdv.n, "ac": sac.n}


def _attn_slots(B, qh_ap, k_ap, v_ap, ctx_dst, first_wait):
    spe, sdv, sac = B.spe, B.sdv, B.sac
    st = B.attn

    for i in range(NSLOT):
        ext = 256 * (i + 1)
        par = i % 2
        ed = st["exp_done"]

        @B.block.tensor
        def _(t, i=i, ext=ext, ed=ed, fw=(first_wait if i == 0 else [])):
            if ed:
                t.wait_ge(sac.h, ed)
            _waits(t, fw)
            lhsT = qh_ap(i)
            off = 0
            while off < ext:
                n = min(512, ext - off)
                t.matmul(B.PSB[:, off:off + n], lhsT, k_ap(off, n),
                         start=True, stop=False, skip_group_check=True)
                off += n
            mm = t.matmul(B.PSB[:, ext - 256:ext], B.identb[:, :],
                          B.AR5[:, 6144 + i * 256:6144 + (i + 1) * 256],
                          start=False, stop=True, skip_group_check=True)
            spe.inc(mm)
        m_sc = spe.n

        @B.block.vector
        def _(v, ext=ext, par=par, m_sc=m_sc, ed2=st["exp_par_%d" % par if ("exp_par_%d" % par) in st else "exp_done"]):
            v.wait_ge(spe.h, m_sc)
            if ed2:
                v.wait_ge(sac.h, ed2)
            sdv.inc(v.reduce_max(B.negm[:, par:par + 1], B.PSB[:, 0:ext],
                                 axis=AX.X, negate=True))
        m_max = sdv.n

        @B.block.scalar
        def _(s, ext=ext, par=par, m_max=m_max, ptf=st["pt_free"]):
            s.wait_ge(sdv.h, m_max)
            if ptf:
                s.wait_ge(spe.h, ptf)
            sac.inc(s.activation(B.AR5[:, 0:ext], B.PSB[:, 0:ext], AF.Exp,
                                 bias=B.negm[:, par:par + 1],
                                 accum_out=B.rsum[:, par:par + 1]))
        st["exp_done"] = sac.n
        st["exp_par_%d" % par] = sac.n
        m_exp = sac.n

        @B.block.vector
        def _(v, par=par, m_exp=m_exp):
            v.wait_ge(sac.h, m_exp)
            sdv.inc(v.reciprocal(B.rrec[:, par:par + 1], B.rsum[:, par:par + 1]))
        m_rec = sdv.n

        ntile = 2 * (i + 1)
        ngrp = (ntile + 3) // 4
        evms = []
        for g in range(ngrp):
            j0, j1 = g * 4, min(g * 4 + 4, ntile)
            bk, war = B.trr.next()

            @B.block.tensor
            def _(t, g=g, j0=j0, j1=j1, bk=bk, war=war, m_exp=m_exp):
                if g == 0:
                    t.wait_ge(sac.h, m_exp)
                _waits(t, war)
                for j in range(j0, j1):
                    mm = t.transpose(B.PST[bk][:, (j - j0) * 128:(j - j0 + 1) * 128],
                                     B.AR5[:, j * 128:(j + 1) * 128],
                                     B.identb[:, :])
                spe.inc(mm)
            m_tr = spe.n

            @B.block.vector
            def _(v, j0=j0, j1=j1, bk=bk, m_tr=m_tr):
                v.wait_ge(spe.h, m_tr)
                sdv.inc(v.tensor_copy(B.AR5[:, 4096 + j0 * 128:4096 + j1 * 128],
                                      B.PST[bk][:, 0:(j1 - j0) * 128]))
            B.trr.done(bk, sdv)
            evms.append(sdv.n)
        st["pt_free"] = spe.n

        m_pv = 0
        for g in range(ngrp):
            j0, j1 = g * 4, min(g * 4 + 4, ntile)

            @B.block.tensor
            def _(t, g=g, j0=j0, j1=j1, ntile=ntile, m_ev=evms[g],
                  cw=(list(st["ctx_evac"]) if g == 0 else [])):
                t.wait_ge(sdv.h, m_ev)
                _waits(t, cw)
                for j in range(j0, j1):
                    mm = t.matmul(B.PSC[:, 0:128],
                                  B.AR5[:, 4096 + j * 128:4096 + (j + 1) * 128],
                                  v_ap(j),
                                  start=(j == 0), stop=(j == ntile - 1),
                                  skip_group_check=True)
                spe.inc(mm)
            m_pv = spe.n

        cb = B._ctmp_cnt % 2
        B._ctmp_cnt += 1

        @B.block.scalar
        def _(s, i=i, par=par, cb=cb, m_pv=m_pv, m_rec=m_rec,
              cf=st["ctmp_free"][cb]):
            s.wait_ge(spe.h, m_pv)
            s.wait_ge(sdv.h, m_rec)
            if cf:
                s.wait_ge(spe.h, cf)
            sac.inc(s.activation(B.AR5[:, 2048 + cb * 128:2048 + (cb + 1) * 128],
                                 B.PSC[:, 0:128], AF.Copy,
                                 scale=B.rrec[:, par:par + 1]))
        m_cev = sac.n
        st["ctx_evac"] = [(sac, sac.n)]

        bk, war = B.trr.next()

        @B.block.tensor
        def _(t, bk=bk, war=war, cb=cb, m_cev=m_cev):
            t.wait_ge(sac.h, m_cev)
            _waits(t, war)
            mm = t.transpose(B.PST[bk][:, 0:128],
                             B.AR5[:, 2048 + cb * 128:2048 + (cb + 1) * 128],
                             B.identb[:, :])
            spe.inc(mm)
        st["ctmp_free"][cb] = spe.n
        m_ctr = spe.n

        @B.block.vector
        def _(v, i=i, bk=bk, m_ctr=m_ctr):
            v.wait_ge(spe.h, m_ctr)
            tns, col = ctx_dst(i)
            sdv.inc(v.tensor_copy(tns[:, col:col + 128], B.PST[bk][:, 0:128]))
        B.trr.done(bk, sdv)


def _p5(B):
    qs, qg, spe, sdv, sac = B.qs, B.qg, B.spe, B.sdv, B.sac
    xn = B.AR1
    C_WQ = [0, 2048]
    C_WK = [4096, 6144]
    C_WV4 = 8192
    C_KH = 16384
    C_V4 = 18432
    C_QH = 26624

    @B.block.sync
    def _(sy):
        sy.wait_ge(sdv.h, B.m_p0["dv"])
        qs.inc(sy.dma_start(out=B.AR5[:, 6144:8192], in_=B.masks_d[:, :]), 16)
    m_masks = qs.n

    prod_mm_done = [0] * 16
    prod_ev_done = [0] * 16
    grp_v4_mm = [0] * 4
    B._s_stored = [0, 0]

    def punit(B, mm_fn, evac_dst, evac_scale, first_ws):
        half, war = B.prr.next()
        c0 = half * 512

        @B.block.tensor
        def _(t, c0=c0, war=war, first_ws=first_ws):
            _waits(t, first_ws)
            _waits(t, war)
            mm_fn(t, c0)
        m_mm = B.spe.n

        @B.block.scalar
        def _(s, c0=c0, m_mm=m_mm):
            s.wait_ge(B.spe.h, m_mm)
            if evac_scale is None:
                B.sac.inc(s.activation(evac_dst, B.PSC[:, c0:c0 + 512], AF.Copy))
            else:
                B.sac.inc(s.activation(evac_dst, B.PSC[:, c0:c0 + 512], AF.Copy,
                                       scale=evac_scale))
        B.prr.done(half, B.sac)

    for h in range(16):
        grp, hh = h // 4, h % 4

        @B.block.gpsimd
        def _(g, h=h, grp=grp):
            if h < 2:
                g.wait_ge(spe.h, B.m_p0["pe"])
            else:
                g.wait_ge(spe.h, prod_mm_done[h - 2])
            qg.inc(g.dma_start(out=B.AR2[:, C_WQ[h % 2]:C_WQ[h % 2] + 2048],
                               in_=B.wqkv_d[h, 0, :, :]), 16)
            qg.inc(g.dma_start(out=B.AR2[:, C_WK[h % 2]:C_WK[h % 2] + 2048],
                               in_=B.wqkv_d[h, 1, :, :]), 16)
            if h % 4 == 0:
                if grp >= 1:
                    g.wait_ge(spe.h, grp_v4_mm[grp - 1])
                qg.inc(g.dma_start(out=B.AR2[:, C_WV4:C_WV4 + 8192],
                                   in_=B.wv4_d[grp, :, :]), 16)
        m_qk = qg.n

        first = [(qg, m_qk)]
        if h == 0:
            first += [(qs, m_masks), (sdv, B.m_p0["dv"]), (sac, B.m_p0["ac"])]
        else:
            first += list(B.attn["ctx_evac"])

        if hh == 0:
            for pos in range(16):
                def mkv(pos):
                    def mmf(t, c0, pos=pos):
                        for fc in range(16):
                            mm = t.matmul(
                                B.PSC[:, c0:c0 + 512],
                                xn[:, fc * 2048 + pos * 128:fc * 2048 + (pos + 1) * 128],
                                B.AR2[:, C_WV4 + fc * 512:C_WV4 + (fc + 1) * 512],
                                start=(fc == 0), stop=(fc == 15))
                        B.spe.inc(mm)
                    return mmf
                punit(B, mkv(pos), B.AR2[:, C_V4 + pos * 512:C_V4 + (pos + 1) * 512],
                      None, first)
                first = []
            grp_v4_mm[grp] = B.spe.n

        for tg in range(4):
            def mkk(tg, h=h):
                def mmf(t, c0, tg=tg, h=h):
                    for fc in range(16):
                        mm = t.matmul(
                            B.PSC[:, c0:c0 + 512],
                            B.AR2[:, C_WK[h % 2] + fc * 128:C_WK[h % 2] + (fc + 1) * 128],
                            xn[:, fc * 2048 + tg * 512:fc * 2048 + (tg + 1) * 512],
                            start=(fc == 0), stop=(fc == 15))
                    B.spe.inc(mm)
                return mmf
            punit(B, mkk(tg), B.AR2[:, C_KH + tg * 512:C_KH + (tg + 1) * 512],
                  None, first)
            first = []

        for qg2 in range(2):
            def mkq(qg2, h=h):
                def mmf(t, c0, qg2=qg2, h=h):
                    for fc in range(16):
                        rhs = bass.AP(B.AR1, fc * 2048 + qg2 * 1024,
                                      [[32768, 128], [256, 4], [1, 128]])
                        mm = t.matmul(
                            B.PSC[:, c0:c0 + 512],
                            B.AR2[:, C_WQ[h % 2] + fc * 128:C_WQ[h % 2] + (fc + 1) * 128],
                            rhs, start=(fc == 0), stop=(fc == 15))
                    B.spe.inc(mm)
                return mmf
            punit(B, mkq(qg2), B.AR2[:, C_QH + qg2 * 512:C_QH + (qg2 + 1) * 512],
                  RSQ, first)
            first = []
        prod_mm_done[h] = B.spe.n
        prod_ev_done[h] = B.sac.n

        _attn_slots(
            B,
            qh_ap=lambda i: B.AR2[:, C_QH + i * 128:C_QH + (i + 1) * 128],
            k_ap=lambda c0, n: B.AR2[:, C_KH + c0:C_KH + c0 + n],
            v_ap=lambda ki, hh=hh: B.AR2[:, C_V4 + ki * 512 + hh * 128:
                                         C_V4 + ki * 512 + (hh + 1) * 128],
            ctx_dst=lambda i, h=h: (B.AR3, h * 1024 + i * 128),
            first_wait=[(sac, prod_ev_done[h])],
        )

    wout_ld = [0] * 4
    outproj_mm = [0] * 4
    for of in range(4):
        @B.block.gpsimd
        def _(g, of=of):
            if of == 0:
                g.wait_ge(spe.h, prod_mm_done[15])
            if of >= 2:
                g.wait_ge(spe.h, outproj_mm[of - 2])
            qg.inc(g.dma_start(out=B.AR2[:, (of % 2) * 8192:(of % 2) * 8192 + 8192],
                               in_=B.wout_d[of, :, :]), 16)
        wout_ld[of] = qg.n

        for i in range(NSLOT):
            half, war = B.prr.next()
            c0 = half * 512

            @B.block.tensor
            def _(t, of=of, i=i, c0=c0, war=war,
                  cw=(list(B.attn["ctx_evac"]) if of == 0 and i == 0 else [])):
                t.wait_ge(qg.h, wout_ld[of])
                _waits(t, cw)
                _waits(t, war)
                dst0 = (of % 2) * 8192
                for cf in range(16):
                    mm = t.matmul(B.PSC[:, c0:c0 + 512],
                                  B.AR3[:, cf * 1024 + i * 128:cf * 1024 + (i + 1) * 128],
                                  B.AR2[:, dst0 + cf * 512:dst0 + (cf + 1) * 512],
                                  start=(cf == 0), stop=(cf == 15))
                spe.inc(mm)
            m_mm = spe.n

            sp = (of * 8 + i) % 2

            @B.block.vector
            def _(v, sp=sp, c0=c0, m_mm=m_mm, sw=B._s_stored[sp]):
                v.wait_ge(spe.h, m_mm)
                if sw:
                    v.wait_ge(qs.h, sw)
                sdv.inc(v.tensor_copy(B.sstg[sp][:, :], B.PSC[:, c0:c0 + 512]))
            B.prr.done(half, sdv)
            m_sev = sdv.n

            @B.block.sync
            def _(sy, of=of, i=i, sp=sp, m_sev=m_sev):
                sy.wait_ge(sdv.h, m_sev)
                qs.inc(sy.dma_start(out=B.sdram[i, :, of * 512:(of + 1) * 512],
                                    in_=B.sstg[sp][:, :]), 16)
            B._s_stored[sp] = qs.n
        outproj_mm[of] = spe.n
    B.m_p5_pe = spe.n


def _p1(B):
    qs, qg, spe, sdv, sac = B.qs, B.qg, B.spe, B.sdv, B.sac
    xn = B.AR1
    C_KIN, C_VIN = 16384, 24576
    ar5f = B.AR5.bitcast(F32)

    @B.block.sync
    def _(sy):
        sy.wait_ge(spe.h, B.m_p5_pe)
        qs.inc(sy.dma_start(out=ar5f[:, 0:1024], in_=B.ropec_d[:, :]), 16)
        qs.inc(sy.dma_start(out=ar5f[:, 1024:2048], in_=B.ropes_d[:, :]), 16)
    B._m_rope = qs.n

    @B.block.vector
    def _(v):
        for fc in range(16):
            sdv.inc(v.tensor_scalar_mul(xn[:, fc * 2048:(fc + 1) * 2048],
                                        xn[:, fc * 2048:(fc + 1) * 2048],
                                        B.n1w[:, fc:fc + 1]))
    m_fold = sdv.n

    blk_free_ms = []
    blk_ld = []

    def load_blk(src_ap, idx):
        half = idx % 2

        @B.block.gpsimd
        def _(g, src_ap=src_ap, idx=idx, half=half):
            if idx == 0:
                g.wait_ge(spe.h, B.m_p5_pe)
            if idx >= 2:
                g.wait_ge(spe.h, blk_free_ms[idx - 2])
            qg.inc(g.dma_start(out=B.AR3[:, half * 8192:half * 8192 + 8192],
                               in_=src_ap), 16)
        blk_ld.append(qg.n)
        return half

    def rope_ap(base_t, col0):
        return bass.AP(base_t, col0, [[512, 128], [128, 4], [1, 64]])

    B._rope_cnt = 0
    B._rope_qbf_free = [0, 0]
    B._rope_qsc_free = [0, 0]

    def rope_and_store(dst_base, dst_stride, m_evac, tblcol, pb):
        @B.block.vector
        def _(v, m_evac=m_evac, tblcol=tblcol, pb=pb, qf=B._rope_qbf_free[pb]):
            v.wait_ge(sac.h, m_evac)
            v.wait_ge(qs.h, B._m_rope)
            if qf:
                v.wait_ge(spe.h, qf)
            cosv = bass.AP(B.AR5.bitcast(F32), tblcol, [[4096, 128], [0, 4], [1, 64]])
            sinv = bass.AP(B.AR5.bitcast(F32), 1024 + tblcol,
                           [[4096, 128], [0, 4], [1, 64]])
            q1, q2 = rope_ap(B.qsc[pb], 0), rope_ap(B.qsc[pb], 64)
            t1, t2 = rope_ap(B.rsc, 0), rope_ap(B.rsc, 64)
            o1, o2 = rope_ap(B.qbf[pb], 0), rope_ap(B.qbf[pb], 64)
            sdv.inc(v.tensor_tensor(t1, q1, cosv, ALU.mult))
            sdv.inc(v.tensor_tensor(t2, q2, sinv, ALU.mult))
            v.wait_ge(sdv.h, sdv.n)
            sdv.inc(v.tensor_tensor(o1, t1, t2, ALU.subtract))
            sdv.inc(v.tensor_tensor(t1, q2, cosv, ALU.mult))
            sdv.inc(v.tensor_tensor(t2, q1, sinv, ALU.mult))
            v.wait_ge(sdv.h, sdv.n)
            sdv.inc(v.tensor_tensor(o2, t1, t2, ALU.add))
        m_rope_done = sdv.n
        B._rope_qsc_free[pb] = m_rope_done

        bk, war = B.trr.next()

        @B.block.tensor
        def _(t, bk=bk, war=war, pb=pb, m_rope_done=m_rope_done):
            t.wait_ge(sdv.h, m_rope_done)
            _waits(t, war)
            for j in range(4):
                mm = t.transpose(B.PST[bk][:, j * 128:(j + 1) * 128],
                                 B.qbf[pb][:, j * 128:(j + 1) * 128], B.identb[:, :])
            spe.inc(mm)
        B._rope_qbf_free[pb] = spe.n
        m_tr = spe.n

        @B.block.vector
        def _(v, bk=bk, m_tr=m_tr, dst_base=dst_base, dst_stride=dst_stride):
            v.wait_ge(spe.h, m_tr)
            dst = bass.AP(B.AR2, dst_base, [[32768, 128], [dst_stride, 4], [1, 128]])
            sdv.inc(v.tensor_copy(dst, B.PST[bk][:, 0:512]))
        B.trr.done(bk, sdv)
        return sdv.n

    for of in range(4):
        half = load_blk(B.wq_d[of, :, :], of)
        ld = blk_ld[-1]
        for i in range(NSLOT):
            phalf, war = B.prr.next()
            c0 = phalf * 512

            @B.block.tensor
            def _(t, of=of, i=i, c0=c0, war=war, half=half, ld=ld):
                t.wait_ge(qg.h, ld)
                if of == 0 and i == 0:
                    t.wait_ge(sdv.h, m_fold)
                _waits(t, war)
                for fc in range(16):
                    mm = t.matmul(
                        B.PSC[:, c0:c0 + 512],
                        xn[:, fc * 2048 + (2 * i) * 128:fc * 2048 + (2 * i + 1) * 128],
                        B.AR3[:, half * 8192 + fc * 512:half * 8192 + (fc + 1) * 512],
                        start=(fc == 0), stop=(fc == 15))
                spe.inc(mm)
            m_mm = spe.n

            pb = B._rope_cnt % 2
            B._rope_cnt += 1

            @B.block.scalar
            def _(s, i=i, c0=c0, pb=pb, m_mm=m_mm, qf=B._rope_qsc_free[pb]):
                s.wait_ge(spe.h, m_mm)
                if qf:
                    s.wait_ge(sdv.h, qf)
                sac.inc(s.activation(B.qsc[pb][:, :], B.PSC[:, c0:c0 + 512], AF.Copy,
                                     scale=B.rstd1q[:, 2 * i:2 * i + 1]))
            B.prr.done(phalf, sac)
            m_ev = sac.n

            rope_and_store(dst_base=(4 * of) * 1024 + i * 128,
                           dst_stride=1024, m_evac=m_ev,
                           tblcol=(2 * i) * 64, pb=pb)
        blk_free_ms.append(spe.n)

    halfk = load_blk(B.wk_d[0, :, :], 4)
    ldk = blk_ld[-1]
    for pos in range(16):
        phalf, war = B.prr.next()
        c0 = phalf * 512

        @B.block.tensor
        def _(t, pos=pos, c0=c0, war=war, halfk=halfk, ldk=ldk):
            t.wait_ge(qg.h, ldk)
            _waits(t, war)
            for fc in range(16):
                mm = t.matmul(
                    B.PSC[:, c0:c0 + 512],
                    xn[:, fc * 2048 + pos * 128:fc * 2048 + (pos + 1) * 128],
                    B.AR3[:, halfk * 8192 + fc * 512:halfk * 8192 + (fc + 1) * 512],
                    start=(fc == 0), stop=(fc == 15))
            spe.inc(mm)
        m_mm = spe.n

        pb = B._rope_cnt % 2
        B._rope_cnt += 1

        @B.block.scalar
        def _(s, pos=pos, c0=c0, pb=pb, m_mm=m_mm, qf=B._rope_qsc_free[pb]):
            s.wait_ge(spe.h, m_mm)
            if qf:
                s.wait_ge(sdv.h, qf)
            sac.inc(s.activation(B.qsc[pb][:, :], B.PSC[:, c0:c0 + 512], AF.Copy,
                                 scale=B.rstd1[:, pos:pos + 1]))
        B.prr.done(phalf, sac)
        m_ev = sac.n

        rope_and_store(dst_base=C_KIN + pos * 128, dst_stride=2048,
                       m_evac=m_ev, tblcol=pos * 64, pb=pb)
    blk_free_ms.append(spe.n)

    halfv = load_blk(B.wv_d[0, :, :], 5)
    ldv = blk_ld[-1]
    for pos in range(16):
        phalf, war = B.prr.next()
        c0 = phalf * 512

        @B.block.tensor
        def _(t, pos=pos, c0=c0, war=war, halfv=halfv, ldv=ldv):
            t.wait_ge(qg.h, ldv)
            _waits(t, war)
            for fc in range(16):
                mm = t.matmul(
                    B.PSC[:, c0:c0 + 512],
                    xn[:, fc * 2048 + pos * 128:fc * 2048 + (pos + 1) * 128],
                    B.AR3[:, halfv * 8192 + fc * 512:halfv * 8192 + (fc + 1) * 512],
                    start=(fc == 0), stop=(fc == 15))
            spe.inc(mm)
        m_mm = spe.n

        @B.block.scalar
        def _(s, pos=pos, c0=c0, m_mm=m_mm):
            s.wait_ge(spe.h, m_mm)
            sac.inc(s.activation(B.AR2[:, C_VIN + pos * 512:C_VIN + (pos + 1) * 512],
                                 B.PSC[:, c0:c0 + 512], AF.Copy,
                                 scale=B.rstd1[:, pos:pos + 1]))
        B.prr.done(phalf, sac)
    B.m_p1_pe = spe.n


def _p2(B):
    qs, qg, spe, sdv, sac = B.qs, B.qg, B.spe, B.sdv, B.sac
    C_KIN, C_VIN = 16384, 24576

    B._m_qkvtap = 0
    if DEBUG_TAPS:
        @B.block.sync
        def _(sy):
            sy.wait_ge(spe.h, B.m_p1_pe)
            sy.wait_ge(sac.h, B.sac.n)
            sy.wait_ge(sdv.h, B.sdv.n)
            qs.inc(sy.dma_start(out=B.qkv_dbg[:, :], in_=B.AR2[:, :]), 16)
        B._m_qkvtap = qs.n

    for h in range(16):
        kv = h // 4
        _attn_slots(
            B,
            qh_ap=lambda i, h=h: B.AR2[:, h * 1024 + i * 128:h * 1024 + (i + 1) * 128],
            k_ap=lambda c0, n, kv=kv: B.AR2[:, C_KIN + kv * 2048 + c0:
                                            C_KIN + kv * 2048 + c0 + n],
            v_ap=lambda ki, kv=kv: B.AR2[:, C_VIN + ki * 512 + kv * 128:
                                         C_VIN + ki * 512 + (kv + 1) * 128],
            ctx_dst=lambda i, h=h: (B.AR3, h * 1024 + i * 128),
            first_wait=[],
        )
    m_heads = spe.n

    B._m_ctxtap = 0
    if DEBUG_TAPS:
        @B.block.sync
        def _(sy):
            sy.wait_ge(sdv.h, B.sdv.n)
            qs.inc(sy.dma_start(out=B.ctx_dbg[:, :], in_=B.AR3[:, :]), 16)
        B._m_ctxtap = qs.n

        @B.block.tensor
        def _(t):
            t.wait_ge(qs.h, B._m_ctxtap)

    wo_mm = [0] * 4
    xn_used = 0
    hadd = 0
    for of in range(4):
        @B.block.gpsimd
        def _(g, of=of):
            if DEBUG_TAPS and of <= 1:
                g.wait_ge(qs.h, B._m_qkvtap)
            if of <= 1:
                g.wait_ge(spe.h, m_heads)
            else:
                g.wait_ge(spe.h, wo_mm[of - 2])
            dst0 = 8192 + (of % 2) * 8192
            qg.inc(g.dma_start(out=B.AR2[:, dst0:dst0 + 8192], in_=B.wo_d[of, :, :]), 16)
        wo_ld = qg.n

        for i in range(NSLOT):
            bi = i % 2

            @B.block.sync
            def _(sy, of=of, i=i, bi=bi, xu=xn_used):
                if of == 0 and i <= 1:
                    sy.wait_ge(spe.h, m_heads)
                if xu:
                    sy.wait_ge(sdv.h, xu)
                qs.inc(sy.dma_start(out=f32v(B.AR2, bi * 512, 512),
                                    in_=B.xfull[2 * i, :, of * 512:(of + 1) * 512]), 16)
            m_x = qs.n

            half, war = B.prr.next()
            c0 = half * 512

            @B.block.tensor
            def _(t, of=of, i=i, c0=c0, war=war, wo_ld=wo_ld):
                t.wait_ge(qg.h, wo_ld)
                _waits(t, war)
                dst0 = 8192 + (of % 2) * 8192
                for cf in range(16):
                    mm = t.matmul(B.PSC[:, c0:c0 + 512],
                                  B.AR3[:, cf * 1024 + i * 128:cf * 1024 + (i + 1) * 128],
                                  B.AR2[:, dst0 + cf * 512:dst0 + (cf + 1) * 512],
                                  start=(cf == 0), stop=(cf == 15))
                spe.inc(mm)
            m_mm = spe.n

            @B.block.vector
            def _(v, i=i, of=of, bi=bi, c0=c0, m_mm=m_mm, m_x=m_x):
                v.wait_ge(qs.h, m_x)
                sdv.inc(v.tensor_scalar(B.xnsc[:, :], f32v(B.AR2, bi * 512, 512),
                                        B.rstd[:, 2 * i:2 * i + 1],
                                        B.bln[:, 2 * i:2 * i + 1],
                                        ALU.mult, ALU.add))
                v.wait_ge(sdv.h, sdv.n)
                v.wait_ge(spe.h, m_mm)
                hdst = f32v(B.AR1, i * 2048 + of * 512, 512)
                sdv.inc(v.tensor_add(hdst, B.PSC[:, c0:c0 + 512], B.xnsc[:, :]))
            B.prr.done(half, sdv)
            hadd = sdv.n
            xn_used = sdv.n
        wo_mm[of] = spe.n
    B.m_p2_pe = spe.n
    B.m_p2_dv = sdv.n


def _p3(B):
    qs, qg, spe, sdv, sac = B.qs, B.qg, B.spe, B.sdv, B.sac
    hv = lambda i, c0, n: f32v(B.AR1, i * 2048 + c0, n)
    C_U = 0
    C_W1 = [16384, 18432]
    C_W3 = [20480, 22528]
    C_W2 = [24576, 28672]
    sqscr = f32v(B.AR5, 0, 2048)

    @B.block.vector
    def _(v):
        for fc in range(16):
            sdv.inc(v.tensor_scalar_mul(B.gatew[:, fc * 8:(fc + 1) * 8],
                                        B.gatew[:, fc * 8:(fc + 1) * 8],
                                        B.n2w[:, fc:fc + 1]))
    m_gw = sdv.n

    utm_free = [0, 0]
    gev = [0, 0]
    gmm = [0, 0]
    m_sd_prev = 0

    for i in range(NSLOT):
        @B.block.vector
        def _(v, i=i, msp=m_sd_prev):
            if msp:
                v.wait_ge(sac.h, msp)
            sdv.inc(v.scalar_tensor_tensor(sqscr, hv(i, 0, 2048), 1.0, hv(i, 0, 2048),
                                           ALU.mult, ALU.mult, accum_out=B.stat[:, 0:1]))
        m_ss = sdv.n

        @B.block.vector
        def _(v, m_ss=m_ss):
            v.wait_ge(sdv.h, m_ss)
            sdv.inc(v.tensor_scalar(B.stat[:, 1:2], B.stat[:, 0:1], 1.0 / 2048.0, EPS,
                                    ALU.mult, ALU.add))
        m_vps = sdv.n

        @B.block.scalar
        def _(s, m_vps=m_vps):
            s.wait_ge(sdv.h, m_vps)
            sac.inc(s.activation(B.stat[:, 1:2], B.stat[:, 1:2], AF.Sqrt))
        m_sd = sac.n
        m_sd_prev = m_sd

        @B.block.vector
        def _(v, i=i, m_sd=m_sd):
            v.wait_ge(sac.h, m_sd)
            sdv.inc(v.reciprocal(B.rstd2[:, i:i + 1], B.stat[:, 1:2]))
        m_r2 = sdv.n

        @B.block.scalar
        def _(s, i=i, m_r2=m_r2, uf=utm_free[i % 2]):
            s.wait_ge(sdv.h, m_r2)
            if uf:
                s.wait_ge(spe.h, uf)
            uc0 = B.utm_c0[i % 2]
            sac.inc(s.activation(B.AR3[:, uc0:uc0 + 2048], hv(i, 0, 2048), AF.Copy,
                                 scale=B.rstd2[:, i:i + 1]))
        m_u = sac.n

        for g in range(4):
            bk, war = B.trr.next()

            @B.block.tensor
            def _(t, i=i, g=g, bk=bk, war=war, m_u=m_u):
                if g == 0:
                    t.wait_ge(sac.h, m_u)
                _waits(t, war)
                uc0 = B.utm_c0[i % 2]
                for j in range(4):
                    fc = g * 4 + j
                    mm = t.transpose(B.PST[bk][:, j * 128:(j + 1) * 128],
                                     B.AR3[:, uc0 + fc * 128:uc0 + (fc + 1) * 128],
                                     B.identb[:, :])
                spe.inc(mm)
            m_tr = spe.n

            @B.block.vector
            def _(v, i=i, g=g, bk=bk, m_tr=m_tr):
                v.wait_ge(spe.h, m_tr)
                for j in range(4):
                    fc = g * 4 + j
                    sdv.inc(v.tensor_scalar_mul(
                        B.AR2[:, C_U + fc * 1024 + i * 128:C_U + fc * 1024 + (i + 1) * 128],
                        B.PST[bk][:, j * 128:(j + 1) * 128],
                        B.n2w[:, fc:fc + 1]))
            B.trr.done(bk, sdv)
        utm_free[i % 2] = spe.n

        gl_done = 0
        for fc in range(16):
            par = fc % 2
            pc0 = par * 512

            @B.block.tensor
            def _(t, i=i, fc=fc, pc0=pc0, gv=gev[par]):
                if gv:
                    t.wait_ge(sdv.h, gv)
                mm = t.transpose(B.PSC[:, pc0:pc0 + 128], hv(i, fc * 128, 128),
                                 B.identf[:, :])
                spe.inc(mm)
            m_gt = spe.n

            @B.block.vector
            def _(v, par=par, pc0=pc0, m_gt=m_gt, gm=gmm[par]):
                v.wait_ge(spe.h, m_gt)
                if gm:
                    v.wait_ge(spe.h, gm)
                sdv.inc(v.tensor_copy(B.gfm[par][:, :], B.PSC[:, pc0:pc0 + 128]))
            gev[par] = sdv.n
            m_gev = sdv.n

            @B.block.tensor
            def _(t, fc=fc, par=par, m_gev=m_gev):
                t.wait_ge(sdv.h, m_gev)
                if fc == 0:
                    t.wait_ge(sdv.h, m_gw)
                mm = t.matmul(B.PSB[:, 0:8], B.gfm[par][:, :],
                              B.gatew[:, fc * 8:(fc + 1) * 8],
                              start=(fc == 0), stop=(fc == 15), skip_group_check=True)
                if fc == 15:
                    spe.inc(mm)
            gmm[par] = spe.n
        gl_done = spe.n

        @B.block.vector
        def _(v, i=i, gl_done=gl_done):
            v.wait_ge(spe.h, gl_done)
            sdv.inc(v.tensor_scalar_mul(B.logits[:, i * 8:(i + 1) * 8], B.PSB[:, 0:8],
                                        B.rstd2[:, i:i + 1]))
    m_logits = sdv.n

    for i in range(NSLOT):
        @B.block.vector
        def _(v, i=i):
            lg = B.logits[:, i * 8:(i + 1) * 8]
            if i == 0:
                v.wait_ge(sdv.h, m_logits)
            sdv.inc(v.reduce_max(B.v1[:, :], lg, axis=AX.X))
            v.wait_ge(sdv.h, sdv.n)
            sdv.inc(v.tensor_scalar(B.m1[:, :], lg, B.v1[:, 0:1], None, ALU.is_equal))
            v.wait_ge(sdv.h, sdv.n)
            sdv.inc(v.scalar_tensor_tensor(B.lgx[:, :], B.m1[:, :], -1e30, lg,
                                           ALU.mult, ALU.add))
            v.wait_ge(sdv.h, sdv.n)
            sdv.inc(v.reduce_max(B.v2[:, :], B.lgx[:, :], axis=AX.X))
            v.wait_ge(sdv.h, sdv.n)
            sdv.inc(v.tensor_scalar(B.m2[:, :], B.lgx[:, :], B.v2[:, 0:1], None,
                                    ALU.is_equal))
            sdv.inc(v.tensor_sub(B.p2c[:, :], B.v2[:, :], B.v1[:, :]))
        m_d = sdv.n

        @B.block.scalar
        def _(s, m_d=m_d):
            s.wait_ge(sdv.h, m_d)
            sac.inc(s.activation(B.p2c[:, :], B.p2c[:, :], AF.Sigmoid))
        m_sig = sac.n

        @B.block.vector
        def _(v, i=i, m_sig=m_sig):
            v.wait_ge(sac.h, m_sig)
            sdv.inc(v.tensor_scalar(B.p1c[:, :], B.p2c[:, :], -1.0, 1.0,
                                    ALU.mult, ALU.add))
            v.wait_ge(sdv.h, sdv.n)
            sdv.inc(v.tensor_scalar_mul(B.wte[:, i * 8:(i + 1) * 8], B.m1[:, :],
                                        B.p1c[:, 0:1]))
            v.wait_ge(sdv.h, sdv.n)
            sdv.inc(v.scalar_tensor_tensor(B.wte[:, i * 8:(i + 1) * 8], B.m2[:, :],
                                           B.p2c[:, 0:1], B.wte[:, i * 8:(i + 1) * 8],
                                           ALU.mult, ALU.add))
    m_wte = sdv.n

    B._m_hdbg = 0
    if DEBUG_TAPS:
        @B.block.sync
        def _(sy):
            sy.wait_ge(sdv.h, B.m_p2_dv)
            for i in range(NSLOT):
                qs.inc(sy.dma_start(out=B.h_dbg[i, :, :],
                                    in_=f32v(B.AR1, i * 2048, 2048)), 16)
        B._m_hdbg = qs.n

    silu = [f32v(B.AR5, 0, 512), f32v(B.AR5, 512, 512)]
    abrot = Rot(2)
    bbrot = Rot(2)
    w1_free = {}
    w2_free = {}
    he_free = [[], []]

    for e in range(NE):
        for qtr in range(4):
            hbuf = qtr % 2
            for ci in range(8):
                c128 = qtr * 8 + ci
                w1h = c128 % 2

                @B.block.gpsimd
                def _(g, e=e, c128=c128, w1h=w1h):
                    if e == 0 and c128 == 0:
                        g.wait_ge(spe.h, B.m_p2_pe)
                    prev = (e, c128 - 2) if c128 >= 2 else (e - 1, 30 + c128)
                    if prev in w1_free:
                        g.wait_ge(spe.h, w1_free[prev])
                    qg.inc(g.dma_start(out=B.AR2[:, C_W1[w1h]:C_W1[w1h] + 2048],
                                       in_=B.w1_d[e, c128, :, :]), 16)
                    qg.inc(g.dma_start(out=B.AR2[:, C_W3[w1h]:C_W3[w1h] + 2048],
                                       in_=B.w3_d[e, c128, :, :]), 16)
                m_wld = qg.n

                for th in range(2):
                    abk, awar = abrot.next()
                    bbk, bwar = bbrot.next()
                    ac0 = abk * 512
                    bc0 = 1024 + bbk * 512

                    @B.block.tensor
                    def _(t, e=e, qtr=qtr, ci=ci, th=th, w1h=w1h, ac0=ac0, bc0=bc0,
                          awar=awar, bwar=bwar, m_wld=m_wld, hbuf=hbuf):
                        t.wait_ge(qg.h, m_wld)
                        if ci == 0 and th == 0:
                            _waits(t, he_free[hbuf])
                            if e == 0 and qtr == 0:
                                t.wait_ge(sdv.h, m_wte)
                        _waits(t, awar)
                        for fc in range(16):
                            mm = t.matmul(
                                B.PSB[:, ac0:ac0 + 512],
                                B.AR2[:, C_W1[w1h] + fc * 128:C_W1[w1h] + (fc + 1) * 128],
                                B.AR2[:, C_U + fc * 1024 + th * 512:
                                      C_U + fc * 1024 + (th + 1) * 512],
                                start=(fc == 0), stop=(fc == 15))
                        spe.inc(mm)
                        _waits(t, bwar)
                        for fc in range(16):
                            mm = t.matmul(
                                B.PSB[:, bc0:bc0 + 512],
                                B.AR2[:, C_W3[w1h] + fc * 128:C_W3[w1h] + (fc + 1) * 128],
                                B.AR2[:, C_U + fc * 1024 + th * 512:
                                      C_U + fc * 1024 + (th + 1) * 512],
                                start=(fc == 0), stop=(fc == 15))
                        spe.inc(mm)
                    m_ab = spe.n

                    @B.block.scalar
                    def _(s, th=th, ac0=ac0, m_ab=m_ab):
                        s.wait_ge(spe.h, m_ab)
                        sac.inc(s.activation(silu[th], B.PSB[:, ac0:ac0 + 512], AF.Silu))
                    abrot.done(abk, sac)
                    m_si = sac.n

                    @B.block.vector
                    def _(v, ci=ci, th=th, bc0=bc0, hbuf=hbuf, m_si=m_si):
                        v.wait_ge(sac.h, m_si)
                        dst = B.AR3[:, hbuf * 8192 + ci * 1024 + th * 512:
                                    hbuf * 8192 + ci * 1024 + (th + 1) * 512]
                        sdv.inc(v.tensor_mul(dst, silu[th], B.PSB[:, bc0:bc0 + 512]))
                    bbrot.done(bbk, sdv)
                w1_free[(e, c128)] = spe.n
            m_he = sdv.n

            for of in range(4):
                w2h = of % 2

                @B.block.gpsimd
                def _(g, e=e, qtr=qtr, of=of, w2h=w2h):
                    if of >= 2:
                        prev = (e, qtr, of - 2)
                    elif qtr >= 1:
                        prev = (e, qtr - 1, 2 + of)
                    elif e >= 1:
                        prev = (e - 1, 3, 2 + of)
                    else:
                        prev = None
                    if prev in w2_free:
                        g.wait_ge(spe.h, w2_free[prev])
                    qg.inc(g.dma_start(out=B.AR2[:, C_W2[w2h]:C_W2[w2h] + 4096],
                                       in_=B.w2_d[e, qtr, of, :, :]), 16)
                m_w2 = qg.n

                for i in range(NSLOT):
                    half, war = B.prr.next()
                    c0 = half * 512

                    @B.block.tensor
                    def _(t, of=of, i=i, c0=c0, war=war, w2h=w2h, hbuf=hbuf,
                          m_w2=m_w2, m_he=m_he):
                        t.wait_ge(qg.h, m_w2)
                        if of == 0 and i == 0:
                            t.wait_ge(sdv.h, m_he)
                        _waits(t, war)
                        for ci in range(8):
                            mm = t.matmul(
                                B.PSC[:, c0:c0 + 512],
                                B.AR3[:, hbuf * 8192 + ci * 1024 + i * 128:
                                      hbuf * 8192 + ci * 1024 + (i + 1) * 128],
                                B.AR2[:, C_W2[w2h] + ci * 512:C_W2[w2h] + (ci + 1) * 512],
                                start=(ci == 0), stop=(ci == 7))
                        spe.inc(mm)
                    m_mm = spe.n

                    @B.block.vector
                    def _(v, e=e, qtr=qtr, i=i, of=of, c0=c0, m_mm=m_mm):
                        v.wait_ge(spe.h, m_mm)
                        if DEBUG_TAPS and e == 0 and qtr == 0 and of == 0 and i == 0:
                            v.wait_ge(qs.h, B._m_hdbg)
                        hdst = f32v(B.AR1, i * 2048 + of * 512, 512)
                        sdv.inc(v.scalar_tensor_tensor(
                            hdst, B.PSC[:, c0:c0 + 512],
                            B.wte[:, i * 8 + e:i * 8 + e + 1],
                            hdst, ALU.mult, ALU.add))
                    B.prr.done(half, sdv)
                w2_free[(e, qtr, of)] = spe.n
            he_free[hbuf] = [(spe, spe.n)]
    B.m_p3_pe = spe.n


def _p4(B):
    qs, spe, sdv, sac = B.qs, B.spe, B.sdv, B.sac
    used = [0] * 8
    stored = [0] * 8
    for i in range(NSLOT):
        bi = i % 2
        xt = f32v(B.AR2, bi * 2048, 2048)
        gt = f32v(B.AR2, 4096 + bi * 2048, 2048)
        ot = f32v(B.AR2, 8192 + bi * 2048, 2048)

        @B.block.sync
        def _(sy, i=i, bi=bi, xt=xt):
            if i < 2:
                sy.wait_ge(spe.h, B.m_p3_pe)
                sy.wait_ge(qs.h, B._s_stored[0])
                sy.wait_ge(qs.h, B._s_stored[1])
            else:
                sy.wait_ge(sdv.h, used[i - 2])
            qs.inc(sy.dma_start(out=xt, in_=B.xfull[2 * i, :, :]), 16)
            qs.inc(sy.dma_start(out=B.sld[bi][:, :], in_=B.sdram[i, :, :]), 16)
        m_x = qs.n

        @B.block.scalar
        def _(s, i=i, gt=gt):
            if i >= 2:
                s.wait_ge(sdv.h, used[i - 2])
            sac.inc(s.activation(gt, f32v(B.AR1, i * 2048, 2048), AF.Silu))
        m_g = sac.n

        @B.block.vector
        def _(v, i=i, bi=bi, xt=xt, gt=gt, ot=ot, m_x=m_x, m_g=m_g):
            v.wait_ge(sac.h, m_g)
            v.wait_ge(qs.h, m_x)
            if i >= 2:
                v.wait_ge(qs.h, stored[i - 2])
            sdv.inc(v.tensor_mul(ot, B.sld[bi][:, :], gt))
            v.wait_ge(sdv.h, sdv.n)
            sdv.inc(v.tensor_add(ot, ot, xt))
        used[i] = sdv.n
        m_out = sdv.n

        @B.block.sync
        def _(sy, i=i, bi=bi, ot=ot, m_out=m_out, m_g=m_g, m_x=m_x):
            sy.wait_ge(sdv.h, m_out)
            qs.inc(sy.dma_start(out=B.yout[i, :, :], in_=ot), 16)
            if DEBUG_TAPS:
                qs.inc(sy.dma_start(out=B.s_dbg[i, :, :], in_=B.sld[bi][:, :]), 16)
                qs.inc(sy.dma_start(out=B.f_dbg[i, :, :],
                                    in_=f32v(B.AR1, i * 2048, 2048)), 16)
        stored[i] = qs.n

    final = qs.n

    @B.block.sync
    def _(sy):
        sy.wait_ge(qs.h, final)



_NC_CACHE = {}


def _get_nc():
    if "nc" not in _NC_CACHE:
        _NC_CACHE["nc"] = build_program()
    return _NC_CACHE["nc"]


def _img_fm(w, cols):
    Dd, C = w.shape
    nb = C // cols
    wr = w.reshape(16, 128, nb, cols)
    out = np.empty((nb, 128, 16 * cols), dtype=np.float32)
    for blk in range(nb):
        out[blk] = wr[:, :, blk, :].transpose(1, 0, 2).reshape(128, 16 * cols)
    return np.ascontiguousarray(out)


def kernel(L0_attn_x, L0_attn_w_qkv, L0_attn_w_out, L0_ffn_norm1_w, L0_ffn_norm2_w,
           L0_ffn_wq, L0_ffn_wk, L0_ffn_wv, L0_ffn_wo, L0_ffn_gate_w,
           L0_ffn_w1, L0_ffn_w3, L0_ffn_w2):
    x = np.ascontiguousarray(np.asarray(L0_attn_x, dtype=np.float32))
    wqkv = np.asarray(L0_attn_w_qkv, dtype=np.float32)
    wout = np.asarray(L0_attn_w_out, dtype=np.float32)
    n1w = np.asarray(L0_ffn_norm1_w, dtype=np.float32)
    n2w = np.asarray(L0_ffn_norm2_w, dtype=np.float32)
    wq = np.asarray(L0_ffn_wq, dtype=np.float32)
    wk = np.asarray(L0_ffn_wk, dtype=np.float32)
    wv = np.asarray(L0_ffn_wv, dtype=np.float32)
    wo = np.asarray(L0_ffn_wo, dtype=np.float32)
    gw = np.asarray(L0_ffn_gate_w, dtype=np.float32)
    w1 = np.asarray(L0_ffn_w1, dtype=np.float32)
    w3 = np.asarray(L0_ffn_w3, dtype=np.float32)
    w2 = np.asarray(L0_ffn_w2, dtype=np.float32)

    wqkv_img = np.empty((16, 3, 128, 2048), dtype=np.float32)
    for h in range(16):
        for m in range(3):
            cols = wqkv[:, m * 2048 + h * 128:m * 2048 + (h + 1) * 128]
            wqkv_img[h, m] = cols.reshape(16, 128, 128).transpose(1, 0, 2).reshape(128, 2048)
    wv4_img = np.empty((4, 128, 8192), dtype=np.float32)
    for g in range(4):
        cols = wqkv[:, 2 * 2048 + g * 512:2 * 2048 + (g + 1) * 512]
        wv4_img[g] = cols.reshape(16, 128, 512).transpose(1, 0, 2).reshape(128, 8192)
    wout_img = _img_fm(wout, 512)
    wq_img = _img_fm(wq, 512)
    wk_img = _img_fm(wk, 512)
    wv_img = _img_fm(wv, 512)
    wo_img = _img_fm(wo, 512)
    w1_img = np.empty((8, 32, 128, 2048), dtype=np.float32)
    w3_img = np.empty((8, 32, 128, 2048), dtype=np.float32)
    for e in range(8):
        w1r = w1[e].reshape(16, 128, 32, 128)
        w3r = w3[e].reshape(16, 128, 32, 128)
        for c in range(32):
            w1_img[e, c] = w1r[:, :, c, :].transpose(1, 0, 2).reshape(128, 2048)
            w3_img[e, c] = w3r[:, :, c, :].transpose(1, 0, 2).reshape(128, 2048)
    w2_img = np.empty((8, 4, 4, 128, 4096), dtype=np.float32)
    w2r = w2.reshape(8, 32, 128, 4, 512)
    for e in range(8):
        for q in range(4):
            for of in range(4):
                blk = w2r[e, q * 8:(q + 1) * 8, :, of, :]
                w2_img[e, q, of] = blk.transpose(1, 0, 2).reshape(128, 4096)
    wqkv_img = np.ascontiguousarray(wqkv_img)
    wv4_img = np.ascontiguousarray(wv4_img)
    w1_img = np.ascontiguousarray(w1_img)
    w3_img = np.ascontiguousarray(w3_img)
    w2_img = np.ascontiguousarray(w2_img)

    n1w_img = np.ascontiguousarray(n1w.reshape(16, 128).T)
    n2w_img = np.ascontiguousarray(n2w.reshape(16, 128).T)
    gw_img = np.ascontiguousarray(
        gw.reshape(16, 128, 8).transpose(1, 0, 2).reshape(128, 128))
    identb = np.eye(128, dtype=ml_dtypes.bfloat16)
    identf = np.eye(128, dtype=np.float32)

    pos = np.arange(2048, dtype=np.float64)
    inv = 1.0 / (1e6 ** (np.arange(0, 128, 2, dtype=np.float64) / 128.0))
    ang = pos[:, None] * inv[None, :]
    cosf = np.cos(ang).astype(np.float32)
    sinf = np.sin(ang).astype(np.float32)

    nc = _get_nc()
    in_maps = []
    for c in range(8):
        b, par = c // 2, c % 2
        perm = np.arange(16) ^ par
        xf = np.ascontiguousarray(x[b].reshape(16, 128, 2048)[perm])
        ropec = np.ascontiguousarray(
            cosf.reshape(16, 128, 64)[perm].transpose(1, 0, 2).reshape(128, 1024))
        ropes = np.ascontiguousarray(
            sinf.reshape(16, 128, 64)[perm].transpose(1, 0, 2).reshape(128, 1024))
        masks = np.zeros((128, 2048), dtype=np.float32)
        tri = np.triu(np.full((128, 128), NEG, dtype=np.float32), 1)
        for i in range(8):
            masks[:, i * 256:i * 256 + 128] = tri
            if par == 0:
                masks[:, i * 256 + 128:i * 256 + 256] = NEG
        masks = masks.astype(ml_dtypes.bfloat16)
        in_maps.append({
            "xfull": xf,
            "identb_d": identb,
            "identf_d": identf,
            "masks_d": masks,
            "ropec_d": ropec,
            "ropes_d": ropes,
            "n1w_d": n1w_img,
            "n2w_d": n2w_img,
            "gatew_d": gw_img,
            "wqkv_d": wqkv_img,
            "wv4_d": wv4_img,
            "wout_d": wout_img,
            "wq_d": wq_img,
            "wk_d": wk_img,
            "wv_d": wv_img,
            "wo_d": wo_img,
            "w1_d": w1_img,
            "w3_d": w3_img,
            "w2_d": w2_img,
        })

    res = run_bass_kernel_spmd(nc, in_maps, core_ids=list(range(8)))

    out = np.empty((4, 2048, 2048), dtype=np.float32)
    for c in range(8):
        b, par = c // 2, c % 2
        y = res.results[c]["yout"]
        for i in range(8):
            g = 2 * i + par
            out[b, g * 128:(g + 1) * 128, :] = y[i]
    if DEBUG_TAPS:
        DEBUG_RES["s"] = np.empty((4, 2048, 2048), dtype=np.float32)
        DEBUG_RES["f"] = np.empty((4, 2048, 2048), dtype=np.float32)
        for c in range(8):
            b, par = c // 2, c % 2
            for i in range(8):
                g = 2 * i + par
                DEBUG_RES["s"][b, g * 128:(g + 1) * 128, :] = \
                    res.results[c]["s_dbg"][i].astype(np.float32)
                DEBUG_RES["f"][b, g * 128:(g + 1) * 128, :] = res.results[c]["f_dbg"][i]
    if DEBUG_TAPS:
        DEBUG_RES["h"] = np.empty((4, 2048, 2048), dtype=np.float32)
        for c in range(8):
            b, par = c // 2, c % 2
            for i in range(8):
                g = 2 * i + par
                DEBUG_RES["h"][b, g * 128:(g + 1) * 128, :] = res.results[c]["h_dbg"][i]
        DEBUG_RES["qkv"] = [res.results[c]["qkv_dbg"].astype(np.float32) for c in range(8)]
        DEBUG_RES["ctx"] = [res.results[c]["ctx_dbg"].astype(np.float32) for c in range(8)]
    return out
